# version 08
# baseline (speedup 1.0000x reference)
import numpy as np
from contextlib import ExitStack

import concourse.bass as bass
import concourse.mybir as mybir
from concourse.bass_utils import run_bass_kernel_spmd

F32 = mybir.dt.float32
BF16 = mybir.dt.bfloat16
AF = mybir.ActivationFunctionType
ALU = mybir.AluOpType

D = 1024
L = 4096
H = 16
DH = 64
CH = 64
NCH = L // CH
HALF = 2048
TA = 512
TB = 128
NP = 8
ROUTE_SCALE = 2.826

STAGE = "full"
DBG = []


def _host_consts():
    c = {}
    c["c_eye"] = np.eye(128, dtype=np.float32)
    tril = np.tril(np.ones((CH, CH), np.float32), -1)
    triu = np.triu(np.ones((CH, CH), np.float32), 1)
    msk = np.zeros((128, 128), np.float32)
    msk[0:64, 0:64] = tril
    msk[64:128, 0:64] = tril
    msk[0:64, 64:128] = triu
    msk[64:128, 64:128] = triu
    c["c_masks"] = msk

    tmat = np.zeros((128, NCH * CH), np.float32)
    atab = np.zeros((128, NCH * CH), np.float32)
    wtab = np.zeros((128, NCH), np.float32)
    coef1 = np.zeros(NCH, np.float64)
    coef2 = np.zeros(NCH, np.float64)
    for i in range(NCH):
        ts = np.float32(i * CH)
        Cf = np.float32(CH)
        idx = (np.arange(1, CH + 1, dtype=np.float32) + ts).astype(np.float32)
        cs = np.cumsum(idx, dtype=np.float32).astype(np.float32)
        weighted_idx = (idx + np.float32(idx.sum()) - cs).astype(np.float32)
        w_i = (weighted_idx / ((1 + ts + Cf) * (ts + Cf) / 2)).astype(np.float32)
        coef1[i] = float((ts / (ts + Cf)) * ((1 + ts) / (1 + ts + Cf)))
        coef2[i] = float((Cf / (ts + Cf)) * ((ts * 2 + 1 + Cf) / (1 + ts + Cf)))
        tri0 = np.float32((1 + ts) * ts / 2)
        a = (tri0 / (tri0 + cs)).astype(np.float32)
        r = cs[:, None]
        j_idx = (cs - idx)[None, :]
        tm = np.where(r >= j_idx, (r - j_idx) / (tri0 + r),
                      np.float32(0.0)).astype(np.float32)
        tmat[0:64, i * CH:(i + 1) * CH] = tm.T
        tmat[64:128, i * CH:(i + 1) * CH] = tm.T
        atab[:, i * CH:(i + 1) * CH] = a[None, :]
        wtab[0:64, i] = w_i
        wtab[64:128, i] = w_i
    c["c_tmat"] = tmat
    c["c_a"] = atab
    c["c_w"] = wtab
    c["coef1"] = coef1
    c["coef2"] = coef2

    selp = np.zeros((16, 8 * 128), np.float32)
    for p in range(8):
        selp[2 * p, p * 128:p * 128 + 64] = 1.0
        selp[2 * p + 1, p * 128 + 64:p * 128 + 128] = 1.0
    c["c_selpair"] = selp
    sele = np.zeros((16, 16 * 128), np.float32)
    for e in range(16):
        sele[e, e * 128:(e + 1) * 128] = 1.0
    c["c_sele"] = sele

    c["c_ones1"] = np.ones((1, 128), np.float32)
    c["c_ones128"] = np.ones((128, 1), np.float32)
    oneblk = np.zeros((128, 2), np.float32)
    oneblk[0:64, 0] = 1.0
    oneblk[64:128, 1] = 1.0
    c["c_oneblk"] = oneblk
    bc2 = np.zeros((2, 128), np.float32)
    bc2[0, 0:64] = 1.0
    bc2[1, 64:128] = 1.0
    c["c_bc2"] = bc2
    c["c_eps"] = np.array([[1e-5, 1e-6]], np.float32).repeat(128, 0)
    return c


def _prep_inputs(ins):
    T = lambda a: np.ascontiguousarray(np.asarray(a).T)
    d = {}
    d["wk"] = T(ins["L0_attn_Wk"])
    d["wq"] = T(ins["L0_attn_Wq"])
    d["wv"] = T(ins["L0_attn_Wv"])
    d["wout"] = T(ins["L0_attn_Wout"])
    d["wbeta"] = T(ins["L0_attn_Wbeta"])
    d["gatew"] = T(ins["L0_ffn_gate_w"])
    d["sgw"] = T(ins["L0_ffn_shared_gate_w"])
    d["suw"] = T(ins["L0_ffn_shared_up_w"])
    d["sdw"] = T(ins["L0_ffn_shared_down_w"])
    d["wg"] = np.ascontiguousarray(np.asarray(ins["L0_ffn_w_gate"]).transpose(0, 2, 1))
    d["wu"] = np.ascontiguousarray(np.asarray(ins["L0_ffn_w_up"]).transpose(0, 2, 1))
    d["wd"] = np.ascontiguousarray(np.asarray(ins["L0_ffn_w_down"]).transpose(0, 2, 1))
    rb = lambda b: np.ascontiguousarray(np.asarray(b).reshape(NP, 128).T)
    d["bk"] = rb(ins["L0_attn_bk"])
    d["bq"] = rb(ins["L0_attn_bq"])
    d["bv"] = rb(ins["L0_attn_bv"])
    d["bout"] = rb(ins["L0_attn_bout"])
    d["bbeta"] = np.ascontiguousarray(np.asarray(ins["L0_attn_bbeta"])[:, None])
    cv = lambda w: np.ascontiguousarray(
        np.asarray(w).reshape(NP, 128, 4).transpose(1, 0, 2).reshape(128, NP * 4))
    d["convk"] = cv(ins["L0_attn_conv_k"])
    d["convq"] = cv(ins["L0_attn_conv_q"])
    d["convv"] = cv(ins["L0_attn_conv_v"])
    rw = np.asarray(ins["L0_attn_rms_w"])
    d["rmsw"] = np.ascontiguousarray(np.concatenate([rw, rw])[:, None])
    d["ebias"] = np.ascontiguousarray(
        np.broadcast_to(np.asarray(ins["L0_ffn_expert_bias"])[None, :],
                        (128, 16)).copy())
    return d


class Sem:
    def __init__(self, nc, es, name):
        self.h = es.enter_context(nc.semaphore(name))
        self.n = 0

    def inc(self, inst, k=1):
        inst.then_inc(self.h, k)
        self.n += k
        return self.n


def build(nc):
    es = ExitStack()
    consts = _host_consts()
    coef1, coef2 = consts["coef1"], consts["coef2"]

    dt_in = {}

    def din(name, shape, dtype=F32):
        dt_in[name] = nc.dram_tensor(name, list(shape), dtype,
                                     kind="ExternalInput").ap()
        return dt_in[name]

    x_d = din("x", (D, L))
    for n in ("wk", "wq", "wv", "wout", "sgw", "suw", "sdw"):
        din(n, (D, D))
    din("wbeta", (D, 16))
    din("gatew", (D, 16))
    for n in ("wg", "wu", "wd"):
        din(n, (16, D, D))
    for n in ("bk", "bq", "bv", "bout"):
        din(n, (128, 8))
    din("bbeta", (16, 1))
    din("ebias", (128, 16))
    din("rmsw", (128, 1))
    for n in ("convk", "convq", "convv"):
        din(n, (128, 32))
    for n, sh in (("c_eye", (128, 128)), ("c_masks", (128, 128)),
                  ("c_tmat", (128, NCH * CH)), ("c_a", (128, NCH * CH)),
                  ("c_selpair", (16, 8 * 128)), ("c_sele", (16, 16 * 128)),
                  ("c_ones1", (1, 128)), ("c_ones128", (128, 1)),
                  ("c_oneblk", (128, 2)), ("c_bc2", (2, 128))):
        din(n, sh, BF16)
    din("c_eyef", (128, 128), F32)
    din("c_ones1f", (1, 128), F32)
    din("c_w", (128, NCH), F32)
    din("c_eps", (128, 2), F32)
    out_d = nc.dram_tensor("out", [D, L], F32, kind="ExternalOutput").ap()

    with es:
        tensors = {}

        def sb(name, shape, dtype, at):
            t = nc.alloc_sbuf_tensor_at(name, list(shape), dtype, offset=at)
            bpe = 2 if dtype == BF16 else 4
            tensors[name] = (at, at + int(np.prod(shape[1:])) * bpe)
            return t

        cur = [16576]

        def sbp(name, shape, dtype):
            t = sb(name, shape, dtype, cur[0])
            cur[0] = (tensors[name][1] + 31) & ~31
            return t

        c_eye = sbp("s_eye", (128, 128), BF16)
        c_eyef = sbp("s_eyef", (128, 128), F32)
        c_masks = sbp("s_masks", (128, 128), BF16)
        c_tmat = sbp("s_tmat", (128, NCH * CH), BF16)
        c_a = sbp("s_a", (128, NCH * CH), BF16)
        c_w = sbp("s_w", (128, NCH), F32)
        c_selpair = sbp("s_selpair", (16, 8 * 128), BF16)
        c_sele = sbp("s_sele", (16, 16 * 128), BF16)
        c_ones1 = sbp("s_ones1", (1, 128), BF16)
        c_ones1f = sbp("s_ones1f", (1, 128), F32)
        c_ones128 = sbp("s_ones128", (128, 1), BF16)
        c_oneblk = sbp("s_oneblk", (128, 2), BF16)
        c_bc2 = sbp("s_bc2", (2, 128), BF16)
        c_eps = sbp("s_eps", (128, 2), F32)
        b_k = sbp("s_bk", (128, 8), F32)
        b_q = sbp("s_bq", (128, 8), F32)
        b_v = sbp("s_bv", (128, 8), F32)
        b_out = sbp("s_bout", (128, 8), F32)
        b_beta = sbp("s_bbeta", (16, 1), F32)
        e_bias = sbp("s_ebias", (128, 16), F32)
        rms_w = sbp("s_rmsw", (128, 1), F32)
        conv_w = {p: sbp(f"s_conv{p}", (128, 32), F32) for p in "kqv"}
        conv_wb = {p: sbp(f"s_convb{p}", (128, 8), F32) for p in "kqv"}
        gatewf = sbp("s_gatewf", (128, 8 * 16), F32)
        wbetaf = sbp("s_wbetaf", (128, 8 * 16), F32)
        wbetab = sbp("s_wbetab", (128, 8 * 16), BF16)
        W_st = [sbp(f"s_W{i}", (128, 512), F32) for i in range(2)]
        WA_st = [sbp(f"s_WA{i}", (128, 512), F32) for i in range(2)]
        carry = {p: sbp(f"s_carry{p}", (128, 24), BF16) for p in "kqv"}
        g_stg = sbp("s_gstg", (128, 1024), BF16)

        y_base = cur[0]
        y_t = sb("s_y", (128, HALF * NP), F32, y_base)
        g_t = sb("s_g", (128, HALF * NP), BF16, y_base)
        cur[0] = (tensors["s_y"][1] + 31) & ~31

        ab = cur[0]
        xn_half = sb("s_xnh", (128, HALF * NP), BF16, ab)
        gb = tensors["s_xnh"][1]
        comb_fb = sb("s_combf", (16, 4 * TA), BF16, gb)
        pa = gb + 4096
        xa_t = sb("s_xa", (128, TA * NP), F32, pa)
        xnf_t = sb("s_xnf", (128, TA * NP), F32, pa + 16384)
        xb_t = sb("s_xb", (128, TA * NP), BF16, pa + 32768)
        xsq_t = sb("s_xsq", (128, TA * NP), BF16, pa + 40960)
        g0 = pa + 49152
        stat_t = sb("s_stat", (1, 5 * TA), F32, g0)
        sc_f = sb("s_scf", (16, TA), F32, g0 + 12288)
        sc_tok = sb("s_sctok", (128, 64), F32, g0 + 14336)
        sel_t = sb("s_sel", (128, 16), F32, g0 + 14592)
        ge_t = sb("s_ge", (128, 32), F32, g0 + 14656)
        rank_t = sb("s_rank", (128, 16), F32, g0 + 14784)
        mask_t = sb("s_mask", (128, 16), F32, g0 + 14848)
        wsel_t = sb("s_wsel", (128, 16), F32, g0 + 14912)
        den_t = sb("s_den", (128, 4), F32, g0 + 14976)
        den_r = sb("s_denr", (128, 4), F32, g0 + 15008)
        comb_t = sb("s_comb", (128, 64), F32, g0 + 15040)
        comb_bs = sb("s_combbs", (128, 64), BF16, g0 + 15296)
        ewg = sb("s_ewg", (128, 8192), BF16, pa)
        h2_t = sb("s_h2", (128, TA * NP), BF16, pa + 16384)
        cb_s = sb("s_cb", (128, TA), BF16, pa + 24576)
        hs_s = sb("s_hs", (128, 2 * TA), BF16, pa + 25600)
        h1_s = sb("s_h1", (128, TA), BF16, pa + 27648)
        ewu = sb("s_ewu", (128, 8192), BF16, pa + 32768)
        ewd = sb("s_ewd", (128, 8192), BF16, pa + 49152)
        stg0 = sb("s_stg0", (128, 1024), F32, pa + 28672)
        stg1 = sb("s_stg1", (128, 1024), F32, pa + 65536)
        stg = (stg0, stg1)
        a_end = pa + 69632

        bcur = [y_base + 32768]

        def sbb(name, shape, dtype):
            t = sb(name, shape, dtype, bcur[0])
            bcur[0] = (tensors[name][1] + 31) & ~31
            return t

        wkb = sbb("s_wkb", (128, 8192), BF16)
        wqb = sbb("s_wqb", (128, 8192), BF16)
        wvb = sbb("s_wvb", (128, 8192), BF16)
        woutb = sbb("s_woutb", (128, 8192), BF16)
        xt_b = sbb("s_xtb", (128, TB * NP), F32)
        xnb = sbb("s_xnb", (128, TB * NP), BF16)
        xbb = sbb("s_xbb", (128, TB * NP), BF16)
        xsqb = sbb("s_xsqb", (128, TB * NP), BF16)
        statB = sbb("s_statB", (1, 5 * TB), F32)
        raw = sbb("s_raw", (128, 2 * (TB + 16)), BF16)
        pr_t = sbb("s_pr", (128, TB * NP), BF16)
        kf_t = sbb("s_kf", (128, TB * NP), BF16)
        qf_t = sbb("s_qf", (128, TB * NP), BF16)
        vf_t = sbb("s_vf", (128, TB * NP), BF16)
        bf_t = sbb("s_bf", (128, TB * NP), BF16)
        ac_t = sbb("s_ac", (128, 2 * TB), BF16)
        ksq_t = sbb("s_ksq", (128, 2 * TB), BF16)
        nrm2 = sbb("s_nrm2", (2, 4 * TB), F32)
        nrm2b = sbb("s_nrm2b", (2, 2 * TB), BF16)
        beta_f = sbb("s_betaf", (16, TB), BF16)
        beta_tok = sbb("s_btok", (128, (TB // 64) * 16), F32)
        NW = TB // 64
        ktok_t = sbb("s_ktok", (128, NW * 512), BF16)
        Rt_w = [sbb(f"s_R{w}", (128, 2 * 1024), BF16) for w in range(NW)]
        P_w = [sbb(f"s_P{w}", (128, 2 * 512), BF16) for w in range(NW)]
        Pt_w = [sbb(f"s_Pt{w}", (128, 2 * 512), BF16) for w in range(NW)]
        Tsb_w = [sbb(f"s_Tsb{w}", (128, 512), BF16) for w in range(NW)]
        Ttsb_w = [sbb(f"s_Ttsb{w}", (128, 512), BF16) for w in range(NW)]
        nTt_w = [sbb(f"s_nTt{w}", (128, 512), BF16) for w in range(NW)]
        Bpf_w = [sbb(f"s_Bpf{w}", (128, 512), BF16) for w in range(NW)]
        Kw_w = [sbb(f"s_Kw{w}", (128, 512), BF16) for w in range(NW)]
        Sm_w = [sbb(f"s_Sm{w}", (128, 512), BF16) for w in range(NW)]
        qa_w = [sbb(f"s_qa{w}", (128, 512), BF16) for w in range(NW)]
        qc_w = [sbb(f"s_qc{w}", (128, 512), BF16) for w in range(NW)]
        U_sb = sbb("s_U", (128, 512), BF16)
        o_store = sbb("s_ost", (128, (TB // 64) * 512), BF16)
        osq_sb = sb("s_osq", (128, 512), BF16, tensors["s_U"][0])
        osq_w1 = sb("s_osq1", (128, 512), BF16, tensors["s_qa0"][0])
        onrm1 = sb("s_onrm1", (2, 512), F32, tensors["s_R1"][0] + 2048)
        onrmb1 = sb("s_onrmb1", (2, 512), BF16, tensors["s_Sm1"][0])
        onrm = sb("s_onrm", (2, 512), F32, tensors["s_R0"][0] + 2048)
        onrmb = sb("s_onrmb", (2, 512), BF16, tensors["s_Sm0"][0])
        osc = sbb("s_osc", (128, (TB // 64) * 512), BF16)
        s_t = sbb("s_st", (128, 2 * TB), F32)
        ot_t = sbb("s_ot", (128, TB * NP), F32)
        W_bf = sbb("s_Wbf", (128, 512), BF16)
        WA_bf = sbb("s_WAbf", (128, 512), BF16)
        wa_tmp = sbb("s_watmp2", (128, 512), F32)
        b_end = bcur[0]
        assert max(a_end, b_end) <= 229376, (a_end, b_end)

        pb = [es.enter_context(nc.psum_tensor(f"pb{i}", [128, 512], F32))
              for i in range(6)]
        tb0 = es.enter_context(nc.psum_tensor("tb0", [128, 1024], BF16))
        tb1 = es.enter_context(nc.psum_tensor("tb1", [128, 1024], BF16))

        sL = Sem(nc, es, "sL")
        sS = Sem(nc, es, "sS")
        sT = Sem(nc, es, "sT")
        sV = Sem(nc, es, "sV")
        sC = Sem(nc, es, "sC")
        sP = Sem(nc, es, "sP")
        sems = {"T": sT, "V": sV, "C": sC, "P": sP, "S": sS, "L": sL}

        blk = es.enter_context(nc.Block())
        TE, VE, SE, PE_, SY = nc.tensor, nc.vector, nc.scalar, nc.gpsimd, nc.sync

        wt_seen = {}

        def wt(eng, key, n=None):
            sem = sems[key] if isinstance(key, str) else key
            n = sem.n if n is None else n
            assert n <= sem.n, (n, sem.n)
            k = (id(eng), id(sem))
            if n > 0 and wt_seen.get(k, 0) < n:
                eng.wait_ge(sem.h, n)
                wt_seen[k] = n

        def load(dst_ap, src_ap):
            SY.dma_start(out=dst_ap, in_=src_ap).then_inc(sL.h, 16)
            sL.n += 16
            return sL.n

        taps = {}

        def tap(name, sbuf_ap, shape, dtype, after):
            if name not in DBG or name in taps:
                return
            d = nc.dram_tensor("dbg_" + name, list(shape), dtype,
                               kind="ExternalOutput").ap()
            taps[name] = True
            wt(SY, after[0], after[1])
            SY.dma_start(out=d, in_=sbuf_ap).then_inc(sS.h, 16)
            sS.n += 16

        for nm, t in (("c_eye", c_eye), ("c_eyef", c_eyef), ("c_masks", c_masks),
                      ("c_tmat", c_tmat), ("c_a", c_a), ("c_w", c_w),
                      ("c_selpair", c_selpair), ("c_sele", c_sele),
                      ("c_ones1", c_ones1), ("c_ones1f", c_ones1f),
                      ("c_ones128", c_ones128), ("c_oneblk", c_oneblk),
                      ("c_bc2", c_bc2), ("c_eps", c_eps),
                      ("bk", b_k), ("bq", b_q), ("bv", b_v), ("bout", b_out),
                      ("bbeta", b_beta), ("ebias", e_bias), ("rmsw", rms_w),
                      ("convk", conv_w["k"]), ("convq", conv_w["q"]),
                      ("convv", conv_w["v"])):
            load(t[:], dt_in[nm])
        load(gatewf[:].rearrange("p (c e) -> p c e", c=8),
             dt_in["gatew"].rearrange("(c p) e -> p c e", p=128))
        load(wbetaf[:].rearrange("p (c e) -> p c e", c=8),
             dt_in["wbeta"].rearrange("(c p) e -> p c e", p=128))

        prefill_c = load(out_d, x_d)
        wt(PE_, "L")
        PE_.memset(W_st[0][:], 0.0)
        PE_.memset(WA_st[0][:], 0.0)
        for p in "kqv":
            PE_.memset(carry[p][:], 0.0)
        for p, bt in (("k", b_k), ("q", b_q), ("v", b_v)):
            PE_.tensor_tensor(
                out=conv_wb[p][:], in0=bt[:],
                in1=conv_w[p][:].rearrange("p (c t) -> p c t", c=8)[:, :, 3],
                op=ALU.mult)
        inst = PE_.tensor_copy(out=wbetab[:], in_=wbetaf[:])
        sP.inc(inst)

        bank_free = {}

        def acq(bank):
            w = bank_free.pop(bank.name, None)
            if w is not None:
                wt(TE, w[0], w[1])

        def rel(bank, key, count):
            bank_free[bank.name] = (key, count)

        def mm_acc(bank_ap, lhsT_list, rhs_list, inc_sem=None, tp=None):
            inst = None
            n = len(lhsT_list)
            for i in range(n):
                inst = TE.matmul(bank_ap, lhsT_list[i], rhs_list[i],
                                 start=(i == 0), stop=(i == n - 1),
                                 tile_position=tp)
            return inc_sem.inc(inst) if inc_sem is not None else inst

        def bcast8(ap2d, blocks=8, width=64):
            return bass.AP(ap2d.tensor, ap2d.offset,
                           [list(ap2d.ap[0]), [0, blocks], [1, width]])

        def cast_dma(dst_ap, src_ap, waits=()):
            for wk in waits:
                if wk is not None:
                    wt(PE_, wk[0], wk[1])
            PE_.dma_start(out=dst_ap, in_=src_ap).then_inc(sL.h, 16)
            sL.n += 16
            return sL.n

        stg_state = {"i": 0, "free": {}}

        def stream_cast(dst_ap, src_ap, nelem, ring, wait_c=None, eng="C"):
            slot = stg_state["i"] % 2
            stg_state["i"] += 1
            fr = stg_state["free"].get((ring[0].name, slot))
            if fr is not None:
                wt(SY, fr[0], fr[1])
            sslice = ring[slot][:, 0:nelem]
            load(sslice, src_ap)
            E = {"C": SE, "P": PE_}[eng]
            S = sems[eng]
            wt(E, "L")
            if wait_c is not None:
                wt(E, wait_c[0], wait_c[1])
            if eng == "C":
                inst = E.copy(out=dst_ap, in_=sslice)
            else:
                inst = E.tensor_copy(out=dst_ap, in_=sslice)
            cnt = S.inc(inst)
            stg_state["free"][(ring[0].name, slot)] = (eng, cnt)
            return cnt

        def emit_ln(xa, xbf, xsqf, stat, Tn, xnf_dst, xn_bf_dst, c_ld,
                    b_sx=None, b_sq=None, b_bc1=None, b_bc2=None):
            b_sx = b_sx if b_sx is not None else pb[0]
            b_sq = b_sq if b_sq is not None else pb[1]
            b_bc1 = b_bc1 if b_bc1 is not None else pb[2]
            b_bc2 = b_bc2 if b_bc2 is not None else pb[3]
            sq_off = Tn if b_sq.name == b_sx.name else 0
            bc_off = Tn if b_bc2.name == b_bc1.name else 0
            wt(SE, "L", c_ld)
            SE.copy(out=xbf[:], in_=xa[:])
            inst = SE.activation(out=xsqf[:], in_=xa[:], func=AF.Square)
            c_sq = sC.inc(inst)
            wt(TE, "C", c_sq)
            acq(b_sx)
            if sq_off == 0:
                acq(b_sq)
            x8 = xbf[:].rearrange("p (c t) -> p c t", c=8)
            q8 = xsqf[:].rearrange("p (c t) -> p c t", c=8)
            mm_acc(b_sx[0:1, 0:Tn], [c_ones128[:]] * 8,
                   [x8[:, c, :] for c in range(8)])
            t_st = mm_acc(b_sq[0:1, sq_off:sq_off + Tn], [c_ones128[:]] * 8,
                          [q8[:, c, :] for c in range(8)], sT)
            mean = stat[0:1, 0:Tn]
            varA = stat[0:1, Tn:2 * Tn]
            msq = stat[0:1, 2 * Tn:3 * Tn]
            sd = stat[0:1, 2 * Tn:3 * Tn]
            rstd = stat[0:1, 3 * Tn:4 * Tn]
            mrs = stat[0:1, 4 * Tn:5 * Tn]
            wt(VE, "T", t_st)
            VE.tensor_scalar(out=mean, in0=b_sx[0:1, 0:Tn], scalar1=1.0 / D,
                             scalar2=None, op0=ALU.mult)
            VE.tensor_scalar(out=varA, in0=b_sq[0:1, sq_off:sq_off + Tn],
                             scalar1=1.0 / D, scalar2=None, op0=ALU.mult)
            VE.tensor_mul(out=msq, in0=mean, in1=mean)
            VE.nop(cycle_cnt=96)
            inst = VE.tensor_sub(out=varA, in0=varA, in1=msq)
            v_var = sV.inc(inst)
            wt(SE, "V", v_var)
            SE.activation(out=sd, in_=varA, func=AF.Ln, bias=c_eps[0:1, 0:1])
            SE.nop(cycle_cnt=128)
            inst = SE.activation(out=rstd, in_=sd, func=AF.Exp, scale=-0.5)
            c_sd = sC.inc(inst)
            wt(VE, "C", c_sd)
            inst = VE.tensor_mul(out=mrs, in0=mean, in1=rstd)
            v_stats = sV.inc(inst)
            rel(b_sx, "V", v_stats)
            if sq_off == 0:
                rel(b_sq, "V", v_stats)
            wt(TE, "V", v_stats)
            acq(b_bc1)
            if bc_off == 0:
                acq(b_bc2)
            TE.matmul(b_bc1[:, 0:Tn], c_ones1f[:], rstd, start=True, stop=True)
            t_bc = sT.inc(TE.matmul(b_bc2[:, bc_off:bc_off + Tn], c_ones1f[:],
                                    mrs, start=True, stop=True))
            wt(VE, "T", t_bc)
            xa8 = xa[:].rearrange("p (c t) -> p c t", c=8)
            if xnf_dst is not None:
                f8 = xnf_dst[:].rearrange("p (c t) -> p c t", c=8)
                for c in range(8):
                    VE.tensor_mul(out=f8[:, c, :], in0=xa8[:, c, :],
                                  in1=b_bc1[:, 0:Tn])
                for c in range(8):
                    inst = VE.tensor_sub(out=f8[:, c, :], in0=f8[:, c, :],
                                         in1=b_bc2[:, bc_off:bc_off + Tn])
                v_xnf = sV.inc(inst)
                rel(b_bc1, "V", v_xnf)
                if bc_off == 0:
                    rel(b_bc2, "V", v_xnf)
                wt(PE_, "V", v_xnf)
                inst = PE_.tensor_copy(
                    out=xn_bf_dst,
                    in_=xnf_dst[:].rearrange("p (c t) -> p c t", c=8))
                return ("P", sP.inc(inst)), v_xnf
            else:
                b8 = xn_bf_dst[:].rearrange("p (c t) -> p c t", c=8)
                scr8 = xsqf[:].rearrange("p (c t) -> p c t", c=8)
                for c in range(8):
                    VE.tensor_mul(out=scr8[:, c, :], in0=xa8[:, c, :],
                                  in1=b_bc1[:, 0:Tn])
                for c in range(8):
                    inst = VE.tensor_sub(out=b8[:, c, :], in0=scr8[:, c, :],
                                         in1=b_bc2[:, bc_off:bc_off + Tn])
                v_xnf = sV.inc(inst)
                rel(b_bc1, "V", v_xnf)
                if bc_off == 0:
                    rel(b_bc2, "V", v_xnf)
                return ("V", v_xnf), v_xnf

        def phase_A(h, b_done):
            first = {"dma": b_done is not None, "y": b_done is not None,
                     "ms": b_done}
            prev_v = prev_c = 0
            for t in range(4):
                tok0 = h * HALF + t * TA
                if first["dma"]:
                    wt(SY, first["ms"][0], first["ms"][1])
                    first["dma"] = False
                if prev_v:
                    wt(SY, "V", prev_v)
                    wt(SY, "C", prev_c)
                    wt(VE, "P")
                c_ld = load(xa_t[:].rearrange("p (c t) -> p c t", c=8),
                            x_d.rearrange("(c p) t -> p c t", p=128)
                            [:, :, tok0:tok0 + TA])
                xn_dst = xn_half[:].rearrange("p (tt c t) -> p tt c t",
                                              tt=4, c=8)[:, t, :, :]
                (pk, pc), v_xnf = emit_ln(xa_t, xb_t, xsq_t, stat_t, TA,
                                          xnf_t, xn_dst, c_ld)
                prev_v, prev_c = v_xnf, sC.n
                acq(pb[4])
                g8 = gatewf[:].rearrange("p (c e) -> p c e", c=8)
                xf8 = xnf_t[:].rearrange("p (c t) -> p c t", c=8)
                wt(TE, "V", v_xnf)
                t_g = mm_acc(pb[4][0:16, 0:TA], [g8[:, c, :] for c in range(8)],
                             [xf8[:, c, :] for c in range(8)], sT)
                wt(SE, "T", t_g)
                inst = SE.activation(out=sc_f[:, 0:TA], in_=pb[4][0:16, 0:TA],
                                     func=AF.Sigmoid)
                c_sc = sC.inc(inst)
                rel(pb[4], "C", c_sc)
                acq(pb[5])
                wt(TE, "C", c_sc)
                insts = None
                for qq in range(4):
                    insts = TE.transpose(pb[5][0:128, qq * 16:(qq + 1) * 16],
                                         sc_f[:, qq * 128:(qq + 1) * 128],
                                         c_eyef[0:16, 0:16])
                t_tr = sT.inc(insts)
                wt(VE, "T", t_tr)
                inst = VE.tensor_copy(out=sc_tok[:], in_=pb[5][0:128, 0:64])
                v_sct = sV.inc(inst)
                rel(pb[5], "V", v_sct)
                for qq in range(4):
                    scq = sc_tok[:, qq * 16:(qq + 1) * 16]
                    inst = VE.tensor_add(out=sel_t[:], in0=scq, in1=e_bias[:])
                    nmid = sV.inc(inst)
                    wt(VE, "V", nmid)
                    inst = None
                    for e in range(16):
                        inst = VE.tensor_scalar(
                            out=ge_t[:, (e % 2) * 16:(e % 2) * 16 + 16],
                            in0=sel_t[:], scalar1=sel_t[:, e:e + 1],
                            scalar2=0.0, op0=ALU.is_le, op1=ALU.add,
                            accum_out=rank_t[:, e:e + 1])
                    nmid = sV.inc(inst)
                    wt(VE, "V", nmid)
                    VE.tensor_scalar(out=mask_t[:], in0=rank_t[:], scalar1=8.5,
                                     scalar2=None, op0=ALU.is_ge)
                    VE.nop(cycle_cnt=96)
                    inst = VE.scalar_tensor_tensor(out=wsel_t[:], in0=scq,
                                                   scalar=1.0,
                                                   in1=mask_t[:], op0=ALU.mult,
                                                   op1=ALU.mult,
                                                   accum_out=den_t[:, qq:qq + 1])
                    nmid = sV.inc(inst)
                    wt(VE, "V", nmid)
                    inst = VE.reciprocal(out=den_r[:, qq:qq + 1],
                                          in_=den_t[:, qq:qq + 1])
                    nmid = sV.inc(inst)
                    wt(VE, "V", nmid)
                    inst = VE.tensor_scalar(
                        out=comb_t[:, qq * 16:(qq + 1) * 16], in0=wsel_t[:],
                        scalar1=den_r[:, qq:qq + 1], scalar2=ROUTE_SCALE,
                        op0=ALU.mult, op1=ALU.mult)
                v_comb = sV.inc(inst)
                wt(PE_, "V", v_comb)
                inst = PE_.tensor_copy(out=comb_bs[:], in_=comb_t[:])
                p_cmb = sP.inc(inst)
                acq(tb0)
                wt(TE, "P", p_cmb)
                insts = None
                for qq in range(4):
                    insts = TE.transpose(tb0[0:16, qq * 128:(qq + 1) * 128],
                                         comb_bs[:, qq * 16:(qq + 1) * 16],
                                         c_eye[:])
                t_ctr = sT.inc(insts)
                wt(VE, "T", t_ctr)
                inst = VE.tensor_copy(out=comb_fb[:, t * TA:(t + 1) * TA],
                                      in_=tb0[0:16, 0:TA])
                v_cfb = sV.inc(inst)
                rel(tb0, "V", v_cfb)
                if t == 0 and h == 0:
                    tap("xn", xn_half[:, 0:4096], (128, 4096), BF16,
                        after=("P", sP.n))
                    tap("scores", sc_tok[:], (128, 64), F32, after=("V", sV.n))
                    tap("comb", comb_t[:], (128, 64), F32, after=("V", sV.n))

            gate_v = sV.n
            gate_p = sP.n

            last_gu_mm = 0
            last_wd_mm = 0
            last_h2 = 0
            hs_free = {0: 0, 1: 0}
            for ei in range(17):
                shared = (ei == 0)
                e = ei - 1
                srcs = (dt_in["sgw"], dt_in["suw"], dt_in["sdw"]) if shared \
                    else (dt_in["wg"][e], dt_in["wu"][e], dt_in["wd"][e])
                g_gate = ("V", gate_v) if ei == 0 else None
                wg_c = cast_dma(
                    ewg[:].rearrange("p (c i) -> p c i", c=8),
                    srcs[0].rearrange("(c p) i -> p c i", p=128),
                    (g_gate, ("T", last_gu_mm) if last_gu_mm else None))
                wu_c = cast_dma(
                    ewu[:].rearrange("p (c i) -> p c i", c=8),
                    srcs[1].rearrange("(c p) i -> p c i", p=128), ())
                wd_c = cast_dma(
                    ewd[:].rearrange("p (c i) -> p c i", c=8),
                    srcs[2].rearrange("(c p) i -> p c i", p=128),
                    (("T", last_wd_mm) if last_wd_mm else None,))

                xh8 = xn_half[:].rearrange("p (tt c t) -> p tt c t", tt=4, c=8)
                for t in range(4):
                    if not shared:
                        acq(pb[4])
                        wt(TE, "V", gate_v)
                        t_cb = sT.inc(TE.matmul(
                            pb[4][:, 0:TA], c_sele[:, e * 128:(e + 1) * 128],
                            comb_fb[:, t * TA:(t + 1) * TA],
                            start=True, stop=True))
                        wt(VE, "T", t_cb)
                        inst = VE.tensor_copy(out=cb_s[:], in_=pb[4][:, 0:TA])
                        v_cb = sV.inc(inst)
                        rel(pb[4], "V", v_cb)
                    h_last = None
                    for ic in range(8):
                        bg = pb[0 + 2 * (ic % 2)]
                        bu = pb[1 + 2 * (ic % 2)]
                        acq(bg); acq(bu)
                        wt(TE, "L", max(wg_c, wu_c))
                        wt(TE, "P", gate_p)
                        if t == 0 and ic == 0 and last_h2:
                            wt(TE, "V", last_h2)
                        g8 = ewg[:].rearrange("p (c i) -> p c i", c=8)
                        u8 = ewu[:].rearrange("p (c i) -> p c i", c=8)
                        mm_acc(bg[:, 0:TA],
                               [g8[:, c, ic * 128:(ic + 1) * 128]
                                for c in range(8)],
                               [xh8[:, t, c, :] for c in range(8)])
                        t_hu = mm_acc(
                            bu[:, 0:TA],
                            [u8[:, c, ic * 128:(ic + 1) * 128] for c in range(8)],
                            [xh8[:, t, c, :] for c in range(8)], sT)
                        if ic == 7 and t == 3:
                            last_gu_mm = t_hu
                        hslot = ic % 2
                        hss = hs_s[:, hslot * TA:(hslot + 1) * TA]
                        wt(SE, "T", t_hu)
                        if hs_free[hslot]:
                            wt(SE, "V", hs_free[hslot])
                        inst = SE.activation(out=hss, in_=bg[:, 0:TA],
                                             func=AF.Silu)
                        c_si = sC.inc(inst)
                        wt(VE, "C", c_si)
                        if shared:
                            inst = VE.tensor_mul(
                                out=h2_t[:, ic * TA:(ic + 1) * TA],
                                in0=hss, in1=bu[:, 0:TA])
                        else:
                            VE.tensor_mul(out=h1_s[:], in0=hss,
                                          in1=bu[:, 0:TA])
                            VE.nop(cycle_cnt=96)
                            inst = VE.tensor_mul(
                                out=h2_t[:, ic * TA:(ic + 1) * TA],
                                in0=h1_s[:], in1=cb_s[:])
                        h_last = sV.inc(inst)
                        hs_free[hslot] = h_last
                        rel(bg, "V", h_last); rel(bu, "V", h_last)
                    last_h2 = h_last
                    wd8 = ewd[:].rearrange("p (c i) -> p c i", c=8)
                    ys = y_t[:].rearrange("p (c tt t) -> p c tt t", c=8, tt=4)
                    for dc in range(8):
                        bd = pb[4 + (dc % 2)]
                        acq(bd)
                        wt(TE, "V", h_last)
                        wt(TE, "L", wd_c)
                        t_d = mm_acc(
                            bd[:, 0:TA],
                            [wd8[:, c, dc * 128:(dc + 1) * 128]
                             for c in range(8)],
                            [h2_t[:, c * TA:(c + 1) * TA] for c in range(8)], sT)
                        if t == 3 and dc == 7:
                            last_wd_mm = t_d
                        wt(VE, "T", t_d)
                        if first["y"]:
                            wt(VE, first["ms"][0], first["ms"][1])
                            first["y"] = False
                        if shared:
                            inst = VE.tensor_copy(out=ys[:, dc, t, :],
                                                  in_=bd[:, 0:TA])
                        else:
                            inst = VE.tensor_add(out=ys[:, dc, t, :],
                                                 in0=ys[:, dc, t, :],
                                                 in1=bd[:, 0:TA])
                        v_y = sV.inc(inst)
                        rel(bd, "V", v_y)
            y_done = sV.n
            wt(SE, "V", y_done)
            SE.activation(out=g_stg[:], in_=y_t[:, 0:1024], func=AF.Silu)
            inst = None
            for r in range(1, 16):
                inst = SE.activation(out=g_t[:, r * 1024:(r + 1) * 1024],
                                     in_=y_t[:, r * 1024:(r + 1) * 1024],
                                     func=AF.Silu)
            c_g0 = sC.inc(inst)
            wt(PE_, "C", c_g0)
            inst = PE_.tensor_copy(out=g_t[:, 0:1024], in_=g_stg[:])
            p_g = sP.inc(inst)
            if h == 0:
                tap("g", g_t[:, 0:4096], (128, 4096), BF16, after=("P", p_g))
            return ("P", p_g), ("T", last_wd_mm)

        def phase_B(h, g_ready, a_done):
            first_dma = True

            def gate_dma():
                nonlocal first_dma
                if first_dma:
                    wt(SY, a_done[0], a_done[1])
                    wt(SY, "V")
                    wt(SY, "P")
                    wt(SY, "C")
                    first_dma = False

            gate_dma()
            wcast = 0
            for mi, (wsrc, wdst) in enumerate(
                    ((dt_in["wk"], wkb), (dt_in["wq"], wqb),
                     (dt_in["wv"], wvb), (dt_in["wout"], woutb))):
                gts = (a_done, ("C", sC.n), ("V", sV.n)) if mi == 0 else ()
                wcast = cast_dma(
                    wdst[:].rearrange("p (c o) -> p c o", c=8),
                    wsrc.rearrange("(c p) o -> p c o", p=128), gts)
            chunk_i0 = h * (HALF // CH)
            wt(PE_, "C")
            wt(PE_, "V")
            PE_.tensor_copy(out=W_bf[:], in_=W_st[0][:])
            inst = PE_.tensor_copy(out=WA_bf[:], in_=WA_st[0][:])
            wbf0 = sP.inc(inst)

            NSUB = HALF // TB
            sh = {"slot_free": {0: 0, 1: 0}, "ksq_free": {0: 0, 1: 0},
                  "vsil_free": {0: 0, 1: 0}, "wfree": {}, "kqvb": 0,
                  "ln_c": 0, "ln_v": 0, "ot_free": 0, "prep": {},
                  "wbf_c": wbf0}
            pslices = [slice(0, 64), slice(64, 128)]
            tps = [(0, 0), (64, 64)]
            p3 = lambda ap: ap.rearrange("q (p z) -> q p z", z=64)

            def pmm(bank, lh, rh, p, par, ss=(True, True)):
                return TE.matmul(
                    bank[pslices[par], p * 64:p * 64 + 64], lh, rh,
                    start=ss[0], stop=ss[1], tile_position=tps[par])

            kf_s = lambda par, p, w: kf_t[pslices[par],
                                          p * TB + w * 64:p * TB + w * 64 + 64]
            qf_s = lambda par, p, w: qf_t[pslices[par],
                                          p * TB + w * 64:p * TB + w * 64 + 64]
            bf_s = lambda par, p, w: bf_t[pslices[par],
                                          p * TB + w * 64:p * TB + w * 64 + 64]

            def front(s_i):
                tok0 = h * HALF + s_i * TB
                if sh["ln_v"]:
                    wt(SY, "V", sh["ln_v"])
                    wt(SY, "C", sh["ln_c"])
                c_ld = load(xt_b[:].rearrange("p (c t) -> p c t", c=8),
                            x_d.rearrange("(c p) t -> p c t", p=128)
                            [:, :, tok0:tok0 + TB])
                (ek, ec), v_xnf = emit_ln(xt_b, xbb, xsqb, statB, TB, None,
                                          xnb, c_ld, pb[4], pb[4], pb[5], pb[5])
                sh["ln_v"] = v_xnf
                sh["ln_c"] = sC.n
                xn8 = xnb[:].rearrange("p (c t) -> p c t", c=8)
                wt(TE, ek, ec)
                wt(TE, "L", wcast)
                yield "F0"

                acq(pb[4])
                wb8 = wbetab[:].rearrange("p (c e) -> p c e", c=8)
                t_b = mm_acc(pb[4][0:16, 0:TB],
                             [wb8[:, c, :] for c in range(8)],
                             [xn8[:, c, :] for c in range(8)], sT)
                wt(SE, "T", t_b)
                inst = SE.activation(out=beta_f[:], in_=pb[4][0:16, 0:TB],
                                     func=AF.Sigmoid, bias=b_beta[:])
                c_bf = sC.inc(inst)
                rel(pb[4], "C", c_bf)
                acq(tb0)
                wt(TE, "C", c_bf)
                insts = None
                for w in range(TB // 64):
                    TE.transpose(tb0[0:64, w * 16:(w + 1) * 16],
                                 beta_f[:, w * 64:(w + 1) * 64],
                                 c_eye[0:16, 0:16], tile_position=(0, 0))
                    insts = TE.transpose(
                        tb0[64:128, 256 + w * 16:256 + (w + 1) * 16],
                        beta_f[:, w * 64:(w + 1) * 64],
                        c_eye[0:16, 0:16], tile_position=(0, 64))
                t_bt = sT.inc(insts)
                wt(VE, "T", t_bt)
                nbt = (TB // 64) * 16
                VE.tensor_copy(out=beta_tok[0:64, :], in_=tb0[0:64, 0:nbt])
                inst = VE.tensor_copy(out=beta_tok[64:128, :],
                                      in_=tb0[64:128, 256:256 + nbt])
                v_bt = sV.inc(inst)
                rel(tb0, "V", v_bt)
                sh["prep"]["v_bt"] = v_bt
                yield "F1"

                v_cv = {}
                pr_cfree = 0
                for pname, wmat, bias_t in (("k", wkb, b_k), ("q", wqb, b_q),
                                            ("v", wvb, b_v)):
                    if pr_cfree:
                        wt(VE, "C", pr_cfree)
                    w8 = wmat[:].rearrange("p (c o) -> p c o", c=8)
                    for oc in range(8):
                        bk_ = pb[4 + oc % 2]
                        acq(bk_)
                        t_p = mm_acc(
                            bk_[:, 0:TB],
                            [w8[:, c, oc * 128:(oc + 1) * 128]
                             for c in range(8)],
                            [xn8[:, c, :] for c in range(8)], sT)
                        slot = oc % 2
                        rs = raw[:, slot * (TB + 16):slot * (TB + 16) + TB + 16]
                        if sh["slot_free"][slot]:
                            wt(PE_, "V", sh["slot_free"][slot])
                        inst = PE_.tensor_copy(
                            out=rs[:, 0:3],
                            in_=carry[pname][:, oc * 3:oc * 3 + 3])
                        p_h = sP.inc(inst)
                        wt(VE, "T", t_p)
                        VE.tensor_scalar(out=rs[:, 3:3 + TB], in0=bk_[:, 0:TB],
                                         scalar1=bias_t[:, oc:oc + 1],
                                         scalar2=None, op0=ALU.add)
                        inst = VE.tensor_scalar(
                            out=ac_t[:, slot * TB:(slot + 1) * TB],
                            in0=bk_[:, 0:TB],
                            scalar1=conv_w[pname][:, oc * 4 + 3:oc * 4 + 4],
                            scalar2=conv_wb[pname][:, oc:oc + 1],
                            op0=ALU.mult, op1=ALU.add)
                        v_ev = sV.inc(inst)
                        rel(bk_, "V", v_ev)
                        wt(VE, "P", p_h)
                        cwp = conv_w[pname]
                        dst = pr_t[:, oc * TB:(oc + 1) * TB]
                        VE.scalar_tensor_tensor(
                            out=ac_t[:, slot * TB:(slot + 1) * TB],
                            in0=rs[:, 2:2 + TB],
                            scalar=cwp[:, oc * 4 + 2:oc * 4 + 3],
                            in1=ac_t[:, slot * TB:(slot + 1) * TB],
                            op0=ALU.mult, op1=ALU.add)
                        VE.nop(cycle_cnt=96)
                        VE.scalar_tensor_tensor(
                            out=ac_t[:, slot * TB:(slot + 1) * TB],
                            in0=rs[:, 1:1 + TB],
                            scalar=cwp[:, oc * 4 + 1:oc * 4 + 2],
                            in1=ac_t[:, slot * TB:(slot + 1) * TB],
                            op0=ALU.mult, op1=ALU.add)
                        VE.nop(cycle_cnt=96)
                        inst = VE.scalar_tensor_tensor(
                            out=dst, in0=rs[:, 0:TB],
                            scalar=cwp[:, oc * 4 + 0:oc * 4 + 1],
                            in1=ac_t[:, slot * TB:(slot + 1) * TB],
                            op0=ALU.mult, op1=ALU.add)
                        v_cv[(pname, oc)] = sV.inc(inst)
                        sh["slot_free"][slot] = v_cv[(pname, oc)]
                        wt(PE_, "V", v_ev)
                        inst = PE_.tensor_copy(
                            out=carry[pname][:, oc * 3:oc * 3 + 3],
                            in_=rs[:, TB:TB + 3])
                        sP.inc(inst)
                    if pname == "k":
                        yield "F2"
                        for op in range(0, 8, 2):
                            nrm_pair(pr_t, kf_t, v_cv, "k", op)
                        pr_cfree = sC.n
                        yield "F2n"
                    elif pname == "q":
                        yield "F3"
                        for op in range(0, 8, 2):
                            nrm_pair(pr_t, qf_t, v_cv, "q", op)
                        pr_cfree = sC.n
                        yield "F3n"
                    else:
                        yield "F4"
                        for oc in range(8):
                            ks = ksq_t[:, (oc % 2) * TB:(oc % 2 + 1) * TB]
                            acq(pb[5])
                            wt(TE, "V", sh["prep"]["v_bt"])
                            t_bb = sT.inc(TE.matmul(
                                pb[5][:, 0:TB],
                                c_selpair[:, oc * 128:(oc + 1) * 128],
                                beta_f[:], start=True, stop=True))
                            wt(SE, "V", v_cv[("v", oc)])
                            if sh["vsil_free"][oc % 2]:
                                wt(SE, "V", sh["vsil_free"][oc % 2])
                            inst = SE.activation(
                                out=ks, in_=pr_t[:, oc * TB:(oc + 1) * TB],
                                func=AF.Silu)
                            c_sv = sC.inc(inst)
                            wt(VE, "T", t_bb)
                            wt(VE, "C", c_sv)
                            if oc == 0:
                                wt(VE, "T", sh["kqvb"])
                            VE.tensor_mul(out=vf_t[:, oc * TB:(oc + 1) * TB],
                                          in0=ks, in1=pb[5][:, 0:TB])
                            inst = VE.tensor_mul(
                                out=bf_t[:, oc * TB:(oc + 1) * TB],
                                in0=kf_t[:, oc * TB:(oc + 1) * TB],
                                in1=pb[5][:, 0:TB])
                            v_vb = sV.inc(inst)
                            sh["vsil_free"][oc % 2] = v_vb
                            rel(pb[5], "V", v_vb)
                        yield "F4n"
                sh["prep"]["v_prep"] = sV.n
                sh["prep"]["ready"] = True

            def nrm_pair(src_t, dst_t, v_cv, pname, op):
                acq(pb[4])
                t_ns = None
                for j in range(2):
                    wt(SE, "V", v_cv[(pname, op + j)])
                    if sh["ksq_free"][j]:
                        wt(SE, "T", sh["ksq_free"][j])
                    inst = SE.activation(
                        out=ksq_t[:, j * TB:(j + 1) * TB],
                        in_=src_t[:, (op + j) * TB:(op + j + 1) * TB],
                        func=AF.Square)
                    c_k2 = sC.inc(inst)
                    wt(TE, "C", c_k2)
                    t_ns = TE.matmul(pb[4][0:2, j * TB:(j + 1) * TB],
                                     c_oneblk[:],
                                     ksq_t[:, j * TB:(j + 1) * TB],
                                     start=True, stop=True)
                t_ns = sT.inc(t_ns)
                sh["ksq_free"][0] = sh["ksq_free"][1] = t_ns
                wt(SE, "T", t_ns)
                SE.activation(out=nrm2[:, 0:2 * TB], in_=pb[4][0:2, 0:2 * TB],
                              func=AF.Sqrt)
                SE.nop(cycle_cnt=128)
                SE.activation(out=nrm2[:, 2 * TB:4 * TB],
                              in_=nrm2[:, 0:2 * TB],
                              func=AF.Ln, bias=c_eps[0:2, 1:2])
                SE.nop(cycle_cnt=128)
                inst = SE.activation(out=nrm2b[:, 0:2 * TB],
                                     in_=nrm2[:, 2 * TB:4 * TB],
                                     func=AF.Exp, scale=-1.0)
                c_nr = sC.inc(inst)
                rel(pb[4], "C", c_nr)
                acq(pb[5])
                wt(TE, "C", c_nr)
                t_b2 = None
                for j in range(2):
                    t_b2 = TE.matmul(pb[5][:, j * TB:(j + 1) * TB], c_bc2[:],
                                     nrm2b[0:2, j * TB:(j + 1) * TB],
                                     start=True, stop=True)
                t_b2 = sT.inc(t_b2)
                wt(VE, "T", t_b2)
                if op == 0:
                    wt(VE, "T", sh["kqvb"])
                inst = None
                for j in range(2):
                    inst = VE.tensor_mul(
                        out=dst_t[:, (op + j) * TB:(op + j + 1) * TB],
                        in0=src_t[:, (op + j) * TB:(op + j + 1) * TB],
                        in1=pb[5][:, j * TB:(j + 1) * TB])
                v_nf = sV.inc(inst)
                rel(pb[5], "V", v_nf)

            def back(s_i):
                v_prep = sh["prep"]["v_prep"]
                stw = [dict() for _ in range(NW)]
                for w in range(NW):
                    tbK = tb0 if w % 2 == 0 else tb1
                    acq(tbK)
                    wt(TE, "V", v_prep)
                    wt(TE, "T", sh["wfree"].get(w, 0))
                    i0_ = None
                    for p in range(8):
                        sl = slice(p * TB + w * 64, p * TB + w * 64 + 64)
                        TE.transpose(tbK[0:64, p * 64:p * 64 + 64],
                                     kf_t[0:64, sl], c_eye[0:64, 0:64],
                                     tile_position=(0, 0))
                        TE.transpose(tbK[64:128, 512 + p * 64:512 + p * 64 + 64],
                                     kf_t[64:128, sl], c_eye[64:128, 64:128],
                                     tile_position=(64, 64))
                    for p in range(8):
                        sl = slice(p * TB + w * 64, p * TB + w * 64 + 64)
                        TE.transpose(tbK[0:64, 512 + p * 64:512 + p * 64 + 64],
                                     vf_t[0:64, sl], c_eye[0:64, 0:64],
                                     tile_position=(0, 0))
                        i0_ = TE.transpose(tbK[64:128, p * 64:p * 64 + 64],
                                           vf_t[64:128, sl],
                                           c_eye[64:128, 64:128],
                                           tile_position=(64, 64))
                    stw[w]["t_tp"] = sT.inc(i0_)
                for w in range(NW):
                    tbK = tb0 if w % 2 == 0 else tb1
                    ci = chunk_i0 + s_i * NW + w
                    stw[w]["ci"] = ci
                    wt(VE, "T", stw[w]["t_tp"])
                    VE.tensor_copy(out=ktok_t[0:64, w * 512:(w + 1) * 512],
                                   in_=tbK[0:64, 0:512])
                    VE.tensor_copy(out=ktok_t[64:128, w * 512:(w + 1) * 512],
                                   in_=tbK[64:128, 512:1024])
                    r0 = Rt_w[w][:, 0:1024]
                    r0v = r0.rearrange("q (p z) -> q p z", z=128)
                    VE.tensor_copy(
                        out=r0v[0:64, :, 0:64],
                        in_=tbK[0:64, 512:1024].rearrange(
                            "q (p z) -> q p z", z=64))
                    inst = VE.tensor_copy(
                        out=r0v[64:128, :, 0:64],
                        in_=tbK[64:128, 0:512].rearrange(
                            "q (p z) -> q p z", z=64))
                    v_tp = sV.inc(inst)
                    rel(tbK, "V", v_tp)
                    stw[w]["r0v"] = r0v
                    for par in range(2):
                        pr = pslices[par]
                        bt0 = beta_tok[pr, w * 16 + par:w * 16 + par + 1]
                        btb = bass.AP(bt0.tensor, bt0.offset,
                                      [list(bt0.ap[0]), [2, 8], [0, 64]])
                        kt3 = ktok_t[pr, w * 512:(w + 1) * 512].rearrange(
                            "q (p z) -> q p z", z=64)
                        inst = VE.tensor_mul(out=r0v[pr, :, 64:128], in0=kt3,
                                             in1=btb)
                    VE.nop(cycle_cnt=96)
                    inst = VE.tensor_scalar(
                        out=Kw_w[w][:], in0=ktok_t[:, w * 512:(w + 1) * 512],
                        scalar1=c_w[:, ci:ci + 1], scalar2=None, op0=ALU.mult)
                    stw[w]["v_r0"] = sV.inc(inst)
                yield "B0"

                for w in range(NW):
                    bT = pb[0] if w % 2 == 0 else pb[2]
                    bTt = pb[1] if w % 2 == 0 else pb[3]
                    acq(bT); acq(bTt)
                    wt(TE, "V", stw[w]["v_r0"])
                    i1_ = None
                    for p in range(8):
                        for par in range(2):
                            pmm(bT, bf_s(par, p, w), kf_s(par, p, w), p, par)
                            i1_ = pmm(bTt, kf_s(par, p, w), bf_s(par, p, w),
                                      p, par)
                    t_T = sT.inc(i1_)
                    wt(VE, "T", t_T)
                    VE.tensor_mul(out=p3(Tsb_w[w][:]), in0=p3(bT[:, :]),
                                  in1=bcast8(c_masks[:, 0:64]))
                    inst = VE.tensor_mul(out=p3(Ttsb_w[w][:]),
                                         in0=p3(bTt[:, :]),
                                         in1=bcast8(c_masks[:, 64:128]))
                    VE.nop(cycle_cnt=96)
                    inst = VE.tensor_scalar(out=nTt_w[w][:], in0=Ttsb_w[w][:],
                                            scalar1=-1.0, scalar2=None,
                                            op0=ALU.mult)
                    v_T = sV.inc(inst)
                    rel(bT, "V", v_T); rel(bTt, "V", v_T)
                    stw[w]["v_sq"] = v_T
                    stw[w]["c_sq"] = 0
                    stw[w]["prevP"] = Tsb_w[w]
                    stw[w]["prevPt"] = Ttsb_w[w]
                    stw[w]["rcur"] = 0
                yield "B1"

                def emit_apply(w, lhsT_tile):
                    S = stw[w]
                    rcur = S["rcur"]
                    rnxt = 1 - rcur
                    if w % 2 == 0:
                        b0a, b0b = pb[4], pb[5]
                        b1a, b1b = pb[0], pb[1]
                    else:
                        b0a, b0b = pb[2], pb[3]
                        b1a, b1b = pb[4], pb[5]
                    wt(TE, "V", max(S["v_sq"], S.get("v_ap", 0)))
                    if S.get("c_ap") or S["c_sq"]:
                        wt(TE, "C", max(S.get("c_ap", 0), S["c_sq"]))
                    i4_ = None
                    for par in range(2):
                        ba, bb_ = (b0a, b0b) if par == 0 else (b1a, b1b)
                        acq(ba); acq(bb_)
                        for p in range(8):
                            bank = ba if p < 4 else bb_
                            reg = bank[pslices[par],
                                       (p % 4) * 128:(p % 4) * 128 + 128]
                            rr = Rt_w[w][pslices[par],
                                         rcur * 1024 + p * 128:
                                         rcur * 1024 + p * 128 + 128]
                            TE.matmul(reg, lhsT_tile[pslices[par],
                                                     p * 64:p * 64 + 64],
                                      rr, start=True, stop=False,
                                      tile_position=tps[par])
                            i4_ = TE.matmul(
                                reg, c_eye[pslices[par], pslices[par]],
                                rr, start=False, stop=True,
                                tile_position=tps[par])
                    t_ap = sT.inc(i4_)
                    wt(SE, "T", t_ap)
                    SE.copy(out=Rt_w[w][0:64, rnxt * 1024:rnxt * 1024 + 512],
                            in_=b0a[0:64, :])
                    inst = SE.copy(
                        out=Rt_w[w][0:64, rnxt * 1024 + 512:rnxt * 1024 + 1024],
                        in_=b0b[0:64, :])
                    c_ap = sC.inc(inst)
                    wt(VE, "T", t_ap)
                    VE.tensor_copy(
                        out=Rt_w[w][64:128, rnxt * 1024:rnxt * 1024 + 512],
                        in_=b1a[64:128, :])
                    inst = VE.tensor_copy(
                        out=Rt_w[w][64:128,
                                    rnxt * 1024 + 512:rnxt * 1024 + 1024],
                        in_=b1b[64:128, :])
                    v_ap = sV.inc(inst)
                    rel(b0a, "C", c_ap); rel(b0b, "C", c_ap)
                    rel(b1a, "V", v_ap); rel(b1b, "V", v_ap)
                    S["c_ap"] = c_ap
                    S["v_ap"] = v_ap
                    S["rcur"] = rnxt

                def emit_sq(w, j):
                    S = stw[w]
                    bP = pb[0] if w % 2 == 0 else pb[2]
                    bPt = pb[1] if w % 2 == 0 else pb[3]
                    acq(bP); acq(bPt)
                    wt(TE, "V", S["v_sq"])
                    if S["c_sq"]:
                        wt(TE, "C", S["c_sq"])
                    dstP = P_w[w][:, (j % 2) * 512:(j % 2) * 512 + 512]
                    dstPt = Pt_w[w][:, (j % 2) * 512:(j % 2) * 512 + 512]
                    i3_ = None
                    for p in range(8):
                        for par in range(2):
                            pmm(bP, S["prevPt"][pslices[par],
                                                p * 64:p * 64 + 64],
                                S["prevP"][pslices[par], p * 64:p * 64 + 64],
                                p, par)
                            i3_ = pmm(bPt, S["prevP"][pslices[par],
                                                      p * 64:p * 64 + 64],
                                      S["prevPt"][pslices[par],
                                                  p * 64:p * 64 + 64],
                                      p, par)
                    t_sq = sT.inc(i3_)
                    wt(SE, "T", t_sq)
                    inst = SE.copy(out=dstP, in_=bP[:, :])
                    S["c_sq"] = sC.inc(inst)
                    wt(VE, "T", t_sq)
                    inst = VE.tensor_copy(out=dstPt, in_=bPt[:, :])
                    S["v_sq"] = sV.inc(inst)
                    rel(bP, "C", S["c_sq"]); rel(bPt, "V", S["v_sq"])
                    S["prevP"] = dstP
                    S["prevPt"] = dstPt

                emit_apply(0, nTt_w[0])
                if NW > 1:
                    emit_apply(1, nTt_w[1])
                yield "B2"
                for j in range(1, 6):
                    for w in range(NW):
                        emit_sq(w, j)
                        emit_apply(w, stw[w]["prevPt"])
                    if j in (2, 4):
                        yield f"B{j + 1}"
                for w in range(NW):
                    assert stw[w]["rcur"] == 0
                yield "B7"

                for w in range(NW):
                    S = stw[w]
                    tbK = tb0 if w % 2 == 0 else tb1
                    acq(tbK)
                    wt(TE, "C", S["c_ap"])
                    wt(TE, "V", S["v_ap"])
                    rfv = Rt_w[w][:, 0:1024].rearrange("q (p z) -> q p z",
                                                       z=128)
                    S["rfv"] = rfv
                    i5_ = None
                    for p in range(8):
                        TE.transpose(tbK[0:64, p * 64:p * 64 + 64],
                                     rfv[0:64, p, 64:128], c_eye[0:64, 0:64],
                                     tile_position=(0, 0))
                        i5_ = TE.transpose(
                            tbK[64:128, 512 + p * 64:512 + p * 64 + 64],
                            rfv[64:128, p, 64:128], c_eye[64:128, 64:128],
                            tile_position=(64, 64))
                    t_bp = sT.inc(i5_)
                    wt(VE, "T", t_bp)
                    VE.tensor_copy(out=Bpf_w[w][0:64, :], in_=tbK[0:64, 0:512])
                    inst = VE.tensor_copy(out=Bpf_w[w][64:128, :],
                                          in_=tbK[64:128, 512:1024])
                    v_bp = sV.inc(inst)
                    rel(tbK, "V", v_bp)
                    S["v_bp"] = v_bp
                    bS = pb[2] if w % 2 == 0 else pb[3]
                    acq(bS)
                    i6_ = None
                    for p in range(8):
                        for par in range(2):
                            i6_ = pmm(bS, kf_s(par, p, w), qf_s(par, p, w),
                                      p, par)
                    t_S = sT.inc(i6_)
                    ci = S["ci"]
                    wt(VE, "T", t_S)
                    inst = VE.tensor_mul(
                        out=p3(Sm_w[w][:]), in0=p3(bS[:, :]),
                        in1=bcast8(c_tmat[:, ci * 64:ci * 64 + 64]))
                    v_S = sV.inc(inst)
                    rel(bS, "V", v_S)
                    qfw3 = qf_t[:].rearrange("q (p t) -> q p t",
                                             t=TB)[:, :, w * 64:w * 64 + 64]
                    VE.tensor_mul(out=p3(qa_w[w][:]), in0=qfw3,
                                  in1=bcast8(c_a[:, ci * 64:ci * 64 + 64]))
                    VE.nop(cycle_cnt=96)
                    inst = VE.tensor_sub(out=p3(qc_w[w][:]), in0=qfw3,
                                         in1=p3(qa_w[w][:]))
                    S["v_q"] = sV.inc(inst)
                sh["kqvb"] = sT.n
                yield "B8"

                c_O = 0
                for w in range(NW):
                    S = stw[w]
                    ci = S["ci"]
                    cf1 = float(coef1[ci])
                    cf2 = float(coef2[ci])
                    rfv = S["rfv"]
                    acq(pb[3])
                    wt(TE, "V", S["v_bp"])
                    wt(TE, "P", sh.get("wbf_c", sP.n))
                    i7_ = None
                    for p in range(8):
                        for par in range(2):
                            i7_ = pmm(pb[3], Bpf_w[w][pslices[par],
                                                      p * 64:p * 64 + 64],
                                      W_bf[pslices[par], p * 64:p * 64 + 64],
                                      p, par)
                    t_U = sT.inc(i7_)
                    wt(VE, "T", t_U)
                    inst = VE.tensor_sub(
                        out=U_sb[:].rearrange("q (p z) -> q p z", z=64),
                        in0=rfv[:, :, 0:64],
                        in1=pb[3][:, :].rearrange("q (p z) -> q p z", z=64))
                    v_U = sV.inc(inst)
                    rel(pb[3], "V", v_U)

                    acq(pb[0]); acq(pb[1])
                    wt(TE, "V", max(v_U, S["v_q"]))
                    i8_ = None
                    for p in range(8):
                        for par in range(2):
                            bo = pb[0] if par == 0 else pb[1]
                            pmm(bo, WA_bf[pslices[par], p * 64:p * 64 + 64],
                                qa_w[w][pslices[par], p * 64:p * 64 + 64],
                                p, par, ss=(True, False))
                            pmm(bo, W_bf[pslices[par], p * 64:p * 64 + 64],
                                qc_w[w][pslices[par], p * 64:p * 64 + 64],
                                p, par, ss=(False, False))
                            i8_ = pmm(bo, U_sb[pslices[par],
                                              p * 64:p * 64 + 64],
                                      Sm_w[w][pslices[par],
                                              p * 64:p * 64 + 64],
                                      p, par, ss=(False, True))
                    t_O = sT.inc(i8_)

                    acq(pb[2]); acq(pb[3])
                    i9_ = None
                    for p in range(8):
                        for par in range(2):
                            pmm(pb[2], ktok_t[pslices[par],
                                              w * 512 + p * 64:
                                              w * 512 + p * 64 + 64],
                                U_sb[pslices[par], p * 64:p * 64 + 64], p, par)
                            i9_ = pmm(pb[3], Kw_w[w][pslices[par],
                                                     p * 64:p * 64 + 64],
                                      U_sb[pslices[par], p * 64:p * 64 + 64],
                                      p, par)
                    t_W = sT.inc(i9_)
                    sh["wfree"][w] = t_W

                    cur_i, nxt_i = ci % 2, (ci + 1) % 2
                    wt(VE, "T", t_W)
                    VE.scalar_tensor_tensor(out=wa_tmp[:], in0=W_st[cur_i][:],
                                            scalar=cf2, in1=pb[3][:, :],
                                            op0=ALU.mult, op1=ALU.add)
                    VE.tensor_add(out=W_st[nxt_i][:], in0=W_st[cur_i][:],
                                  in1=pb[2][:, :])
                    inst = VE.scalar_tensor_tensor(out=WA_st[nxt_i][:],
                                                   in0=WA_st[cur_i][:],
                                                   scalar=cf1, in1=wa_tmp[:],
                                                   op0=ALU.mult, op1=ALU.add)
                    v_st = sV.inc(inst)
                    rel(pb[2], "V", v_st); rel(pb[3], "V", v_st)
                    wt(PE_, "V", v_st)
                    PE_.tensor_copy(out=W_bf[:], in_=W_st[nxt_i][:])
                    inst = PE_.tensor_copy(out=WA_bf[:], in_=WA_st[nxt_i][:])
                    sh["wbf_c"] = sP.inc(inst)

                    wt(SE, "T", t_O)
                    SE.copy(out=o_store[0:64, w * 512:(w + 1) * 512],
                            in_=pb[0][0:64, :])
                    inst = SE.copy(out=o_store[64:128, w * 512:(w + 1) * 512],
                                   in_=pb[1][64:128, :])
                    c_O = sC.inc(inst)
                    rel(pb[0], "C", c_O); rel(pb[1], "C", c_O)
                    yield f"B9.{w}"

                rms_banks = {0: (pb[2], pb[3]), 1: (pb[0], pb[1])}
                osq2 = {0: osq_sb, 1: osq_w1}
                onr2 = {0: onrm, 1: onrm1}
                onb2 = {0: onrmb, 1: onrmb1}
                WL = list(range(TB // 64))
                cq = {}
                tn_ = {}
                cn_ = {}
                tb_ = {}
                wt(SE, "T", sh["wfree"].get(NW - 1, 0))
                for w in WL:
                    wt(SE, "C", c_O)
                    if sh.get("rms_free"):
                        wt(SE, "V", sh["rms_free"])
                    inst = SE.activation(out=osq2[w][:],
                                         in_=o_store[:, w * 512:(w + 1) * 512],
                                         func=AF.Square)
                    cq[w] = sC.inc(inst)
                for w in WL:
                    bs, bb_ = rms_banks[w]
                    acq(bs)
                    wt(TE, "C", cq[w])
                    i10 = None
                    for p in range(8):
                        i10 = TE.matmul(bs[0:2, p * 64:p * 64 + 64],
                                        c_oneblk[:],
                                        osq2[w][:, p * 64:p * 64 + 64],
                                        start=True, stop=True)
                    tn_[w] = sT.inc(i10)
                for w in WL:
                    bs, bb_ = rms_banks[w]
                    wt(SE, "T", tn_[w])
                    SE.activation(out=onr2[w][:, 0:512], in_=bs[0:2, 0:512],
                                  func=AF.Ln, scale=1.0 / DH,
                                  bias=c_eps[0:2, 1:2])
                    SE.nop(cycle_cnt=128)
                    inst = SE.activation(out=onb2[w][:, 0:512],
                                         in_=onr2[w][:, 0:512],
                                         func=AF.Exp, scale=-0.5)
                    cn_[w] = sC.inc(inst)
                    rel(bs, "C", cn_[w])
                for w in WL:
                    bs, bb_ = rms_banks[w]
                    acq(bb_)
                    wt(TE, "C", cn_[w])
                    i11 = None
                    for p in range(8):
                        i11 = TE.matmul(bb_[:, p * 64:p * 64 + 64], c_bc2[:],
                                        onb2[w][0:2, p * 64:p * 64 + 64],
                                        start=True, stop=True)
                    tb_[w] = sT.inc(i11)
                v_os = 0
                for w in WL:
                    bs, bb_ = rms_banks[w]
                    wt(VE, "T", tb_[w])
                    VE.tensor_mul(out=osq2[w][:],
                                  in0=o_store[:, w * 512:(w + 1) * 512],
                                  in1=bb_[:, :])
                    VE.nop(cycle_cnt=96)
                    inst = VE.tensor_scalar(out=osc[:, w * 512:(w + 1) * 512],
                                            in0=osq2[w][:],
                                            scalar1=rms_w[:, 0:1],
                                            scalar2=None, op0=ALU.mult)
                    v_os = sV.inc(inst)
                    rel(bb_, "V", v_os)
                sh["rms_free"] = v_os
                yield "B11"

                wo8 = woutb[:].rearrange("p (c o) -> p c o", c=8)
                oscv = osc[:].rearrange("q (w2 p c2) -> q p w2 c2",
                                        w2=TB // 64, c2=64)
                v_out = 0
                v_osl = sV.n
                tok0 = h * HALF + s_i * TB
                for oc in range(8):
                    bk_ = pb[oc % 2]
                    acq(bk_)
                    wt(TE, "V", v_osl)
                    t_s = mm_acc(
                        bk_[:, 0:TB],
                        [wo8[:, c, oc * 128:(oc + 1) * 128] for c in range(8)],
                        [oscv[:, c, :, :] for c in range(8)], sT)
                    slot = oc % 2
                    wt(VE, "T", t_s)
                    wt(VE, g_ready[0], g_ready[1])
                    if oc == 0 and sh["ot_free"]:
                        wt(VE, "S", sh["ot_free"])
                    inst = VE.tensor_scalar(
                        out=s_t[:, slot * TB:(slot + 1) * TB],
                        in0=bk_[:, 0:TB], scalar1=b_out[:, oc:oc + 1],
                        scalar2=None, op0=ALU.add)
                    v_se = sV.inc(inst)
                    rel(bk_, "V", v_se)
                    VE.nop(cycle_cnt=96)
                    gsl = g_t[:, oc * HALF + s_i * TB:oc * HALF + s_i * TB + TB]
                    inst = VE.tensor_mul(out=ot_t[:, oc * TB:(oc + 1) * TB],
                                         in0=s_t[:, slot * TB:(slot + 1) * TB],
                                         in1=gsl)
                    v_out = sV.inc(inst)
                if s_i == 0 and h == 0:
                    tap("s0", s_t[:], (128, 2 * TB), F32, after=("V", sV.n))
                wt(PE_, "V", v_out)
                wt(PE_, "L", prefill_c)
                PE_.dma_start(
                    out=out_d.rearrange("(c p) t -> p c t", p=128)
                    [:, :, tok0:tok0 + TB],
                    in_=ot_t[:].rearrange("p (c t) -> p c t", c=8),
                    accum_op=ALU.add).then_inc(sS.h, 16)
                sS.n += 16
                sh["ot_free"] = sS.n
                yield "B13"

            NW_local = NW
            fgen = front(0)
            for _ in fgen:
                pass
            for s_i in range(NSUB):
                bgen = back(s_i)
                fgen = front(s_i + 1) if s_i + 1 < NSUB else None
                plan = ["B0", "F0", "B1", "F1", "B2", "B3", "B5", "B7",
                        "F2", "B8", "F2n", "F3", "B9.0", "F3n", "F4",
                        "B9.1", "F4n", "B11", "B13"]
                bdone = fdone = False
                for stage in plan:
                    if stage.startswith("F"):
                        if fgen is None or fdone:
                            continue
                        try:
                            got = next(fgen)
                        except StopIteration:
                            fdone = True
                    else:
                        if bdone:
                            continue
                        try:
                            got = next(bgen)
                        except StopIteration:
                            bdone = True
                while not bdone:
                    try:
                        next(bgen)
                    except StopIteration:
                        bdone = True
                if fgen is not None and not fdone:
                    for _ in fgen:
                        pass
            return ("S", sS.n)

        b_done = None
        for h in range(2):
            g_ready, a_done = phase_A(h, b_done)
            if STAGE == "A":
                ga = nc.dram_tensor(f"gA{h}", [128, HALF * NP], BF16,
                                    kind="ExternalOutput").ap()
                wt(SY, "P")
                SY.dma_start(out=ga, in_=g_t[:]).then_inc(sS.h, 16)
                sS.n += 16
                xna = nc.dram_tensor(f"xnA{h}", [128, HALF * NP], BF16,
                                     kind="ExternalOutput").ap()
                SY.dma_start(out=xna, in_=xn_half[:]).then_inc(sS.h, 16)
                sS.n += 16
                cfa = nc.dram_tensor(f"cfA{h}", [16, 4 * TA], BF16,
                                     kind="ExternalOutput").ap()
                SY.dma_start(out=cfa, in_=comb_fb[:]).then_inc(sS.h, 16)
                sS.n += 16
                b_done = ("S", sS.n)
            else:
                b_done = phase_B(h, g_ready, a_done)

        if STAGE == "A":
            wt(SY, "S")
            SY.dma_start(out=out_d, in_=x_d).then_inc(sS.h, 16)
            sS.n += 16
        wt(SY, "S")
    return nc


def kernel(**inputs):
    x = np.asarray(inputs["L0_attn_x"])
    host = _prep_inputs(inputs)
    consts = _host_consts()
    import ml_dtypes
    cin = {}
    for n in ("c_eye", "c_masks", "c_tmat", "c_a", "c_selpair", "c_sele",
              "c_ones1", "c_ones128", "c_oneblk", "c_bc2"):
        cin[n] = consts[n].astype(ml_dtypes.bfloat16)
    cin["c_eyef"] = consts["c_eye"]
    cin["c_ones1f"] = consts["c_ones1"]
    cin["c_w"] = consts["c_w"]
    cin["c_eps"] = consts["c_eps"]

    nc = bass.Bass("TRN2", target_bir_lowering=False)
    build(nc)
    in_maps = []
    for b in range(8):
        m = {"x": np.ascontiguousarray(x[b].T)}
        m.update(host)
        m.update(cin)
        in_maps.append(m)
    res = run_bass_kernel_spmd(nc, in_maps, core_ids=list(range(8)))
    out = np.stack([np.asarray(r["out"]).T for r in res.results], axis=0)
    return out


# version 09
# speedup vs baseline: 1.0373x; 1.0373x over previous
import numpy as np
from contextlib import ExitStack

import concourse.bass as bass
import concourse.mybir as mybir
from concourse.bass_utils import run_bass_kernel_spmd

F32 = mybir.dt.float32
BF16 = mybir.dt.bfloat16
AF = mybir.ActivationFunctionType
ALU = mybir.AluOpType

D = 1024
L = 4096
H = 16
DH = 64
CH = 64
NCH = L // CH
HALF = 2048
TA = 512
TB = 128
NP = 8
ROUTE_SCALE = 2.826

STAGE = "full"
DBG = []


def _host_consts():
    c = {}
    c["c_eye"] = np.eye(128, dtype=np.float32)
    tril = np.tril(np.ones((CH, CH), np.float32), -1)
    triu = np.triu(np.ones((CH, CH), np.float32), 1)
    msk = np.zeros((128, 128), np.float32)
    msk[0:64, 0:64] = tril
    msk[64:128, 0:64] = tril
    msk[0:64, 64:128] = triu
    msk[64:128, 64:128] = triu
    c["c_masks"] = msk

    tmat = np.zeros((128, NCH * CH), np.float32)
    atab = np.zeros((128, NCH * CH), np.float32)
    wtab = np.zeros((128, NCH), np.float32)
    coef1 = np.zeros(NCH, np.float64)
    coef2 = np.zeros(NCH, np.float64)
    for i in range(NCH):
        ts = np.float32(i * CH)
        Cf = np.float32(CH)
        idx = (np.arange(1, CH + 1, dtype=np.float32) + ts).astype(np.float32)
        cs = np.cumsum(idx, dtype=np.float32).astype(np.float32)
        weighted_idx = (idx + np.float32(idx.sum()) - cs).astype(np.float32)
        w_i = (weighted_idx / ((1 + ts + Cf) * (ts + Cf) / 2)).astype(np.float32)
        coef1[i] = float((ts / (ts + Cf)) * ((1 + ts) / (1 + ts + Cf)))
        coef2[i] = float((Cf / (ts + Cf)) * ((ts * 2 + 1 + Cf) / (1 + ts + Cf)))
        tri0 = np.float32((1 + ts) * ts / 2)
        a = (tri0 / (tri0 + cs)).astype(np.float32)
        r = cs[:, None]
        j_idx = (cs - idx)[None, :]
        tm = np.where(r >= j_idx, (r - j_idx) / (tri0 + r),
                      np.float32(0.0)).astype(np.float32)
        tmat[0:64, i * CH:(i + 1) * CH] = tm.T
        tmat[64:128, i * CH:(i + 1) * CH] = tm.T
        atab[:, i * CH:(i + 1) * CH] = a[None, :]
        wtab[0:64, i] = w_i
        wtab[64:128, i] = w_i
    c["c_tmat"] = tmat
    c["c_a"] = atab
    c["c_w"] = wtab
    c["coef1"] = coef1
    c["coef2"] = coef2

    selp = np.zeros((16, 8 * 128), np.float32)
    for p in range(8):
        selp[2 * p, p * 128:p * 128 + 64] = 1.0
        selp[2 * p + 1, p * 128 + 64:p * 128 + 128] = 1.0
    c["c_selpair"] = selp
    sele = np.zeros((16, 16 * 128), np.float32)
    for e in range(16):
        sele[e, e * 128:(e + 1) * 128] = 1.0
    c["c_sele"] = sele

    c["c_ones1"] = np.ones((1, 128), np.float32)
    c["c_ones128"] = np.ones((128, 1), np.float32)
    oneblk = np.zeros((128, 2), np.float32)
    oneblk[0:64, 0] = 1.0
    oneblk[64:128, 1] = 1.0
    c["c_oneblk"] = oneblk
    bc2 = np.zeros((2, 128), np.float32)
    bc2[0, 0:64] = 1.0
    bc2[1, 64:128] = 1.0
    c["c_bc2"] = bc2
    c["c_eps"] = np.array([[1e-5, 1e-6]], np.float32).repeat(128, 0)
    return c


def _prep_inputs(ins):
    T = lambda a: np.ascontiguousarray(np.asarray(a).T)
    d = {}
    d["wk"] = T(ins["L0_attn_Wk"])
    d["wq"] = T(ins["L0_attn_Wq"])
    d["wv"] = T(ins["L0_attn_Wv"])
    d["wout"] = T(ins["L0_attn_Wout"])
    d["wbeta"] = T(ins["L0_attn_Wbeta"])
    d["gatew"] = T(ins["L0_ffn_gate_w"])
    d["sgw"] = T(ins["L0_ffn_shared_gate_w"])
    d["suw"] = T(ins["L0_ffn_shared_up_w"])
    d["sdw"] = T(ins["L0_ffn_shared_down_w"])
    d["wg"] = np.ascontiguousarray(np.asarray(ins["L0_ffn_w_gate"]).transpose(0, 2, 1))
    d["wu"] = np.ascontiguousarray(np.asarray(ins["L0_ffn_w_up"]).transpose(0, 2, 1))
    d["wd"] = np.ascontiguousarray(np.asarray(ins["L0_ffn_w_down"]).transpose(0, 2, 1))
    rb = lambda b: np.ascontiguousarray(np.asarray(b).reshape(NP, 128).T)
    d["bk"] = rb(ins["L0_attn_bk"])
    d["bq"] = rb(ins["L0_attn_bq"])
    d["bv"] = rb(ins["L0_attn_bv"])
    d["bout"] = rb(ins["L0_attn_bout"])
    d["bbeta"] = np.ascontiguousarray(np.asarray(ins["L0_attn_bbeta"])[:, None])
    cv = lambda w: np.ascontiguousarray(
        np.asarray(w).reshape(NP, 128, 4).transpose(1, 0, 2).reshape(128, NP * 4))
    d["convk"] = cv(ins["L0_attn_conv_k"])
    d["convq"] = cv(ins["L0_attn_conv_q"])
    d["convv"] = cv(ins["L0_attn_conv_v"])
    rw = np.asarray(ins["L0_attn_rms_w"])
    d["rmsw"] = np.ascontiguousarray(np.concatenate([rw, rw])[:, None])
    d["ebias"] = np.ascontiguousarray(
        np.broadcast_to(np.asarray(ins["L0_ffn_expert_bias"])[None, :],
                        (128, 16)).copy())
    return d


class Sem:
    def __init__(self, nc, es, name):
        self.h = es.enter_context(nc.semaphore(name))
        self.n = 0

    def inc(self, inst, k=1):
        inst.then_inc(self.h, k)
        self.n += k
        return self.n


def build(nc):
    es = ExitStack()
    consts = _host_consts()
    coef1, coef2 = consts["coef1"], consts["coef2"]

    dt_in = {}

    def din(name, shape, dtype=F32):
        dt_in[name] = nc.dram_tensor(name, list(shape), dtype,
                                     kind="ExternalInput").ap()
        return dt_in[name]

    x_d = din("x", (D, L))
    for n in ("wk", "wq", "wv", "wout", "sgw", "suw", "sdw"):
        din(n, (D, D))
    din("wbeta", (D, 16))
    din("gatew", (D, 16))
    for n in ("wg", "wu", "wd"):
        din(n, (16, D, D))
    for n in ("bk", "bq", "bv", "bout"):
        din(n, (128, 8))
    din("bbeta", (16, 1))
    din("ebias", (128, 16))
    din("rmsw", (128, 1))
    for n in ("convk", "convq", "convv"):
        din(n, (128, 32))
    for n, sh in (("c_eye", (128, 128)), ("c_masks", (128, 128)),
                  ("c_tmat", (128, NCH * CH)), ("c_a", (128, NCH * CH)),
                  ("c_selpair", (16, 8 * 128)), ("c_sele", (16, 16 * 128)),
                  ("c_ones1", (1, 128)), ("c_ones128", (128, 1)),
                  ("c_oneblk", (128, 2)), ("c_bc2", (2, 128))):
        din(n, sh, BF16)
    din("c_eyef", (128, 128), F32)
    din("c_ones1f", (1, 128), F32)
    din("c_w", (128, NCH), F32)
    din("c_eps", (128, 2), F32)
    out_d = nc.dram_tensor("out", [D, L], F32, kind="ExternalOutput").ap()

    with es:
        tensors = {}

        def sb(name, shape, dtype, at):
            t = nc.alloc_sbuf_tensor_at(name, list(shape), dtype, offset=at)
            bpe = 2 if dtype == BF16 else 4
            tensors[name] = (at, at + int(np.prod(shape[1:])) * bpe)
            return t

        cur = [16576]

        def sbp(name, shape, dtype):
            t = sb(name, shape, dtype, cur[0])
            cur[0] = (tensors[name][1] + 31) & ~31
            return t

        c_eye = sbp("s_eye", (128, 128), BF16)
        c_eyef = sbp("s_eyef", (128, 128), F32)
        c_masks = sbp("s_masks", (128, 128), BF16)
        c_tmat = sbp("s_tmat", (128, NCH * CH), BF16)
        c_a = sbp("s_a", (128, NCH * CH), BF16)
        c_w = sbp("s_w", (128, NCH), F32)
        c_selpair = sbp("s_selpair", (16, 8 * 128), BF16)
        c_sele = sbp("s_sele", (16, 16 * 128), BF16)
        c_ones1 = sbp("s_ones1", (1, 128), BF16)
        c_ones1f = sbp("s_ones1f", (1, 128), F32)
        c_ones128 = sbp("s_ones128", (128, 1), BF16)
        c_oneblk = sbp("s_oneblk", (128, 2), BF16)
        c_bc2 = sbp("s_bc2", (2, 128), BF16)
        c_eps = sbp("s_eps", (128, 2), F32)
        b_k = sbp("s_bk", (128, 8), F32)
        b_q = sbp("s_bq", (128, 8), F32)
        b_v = sbp("s_bv", (128, 8), F32)
        b_out = sbp("s_bout", (128, 8), F32)
        b_beta = sbp("s_bbeta", (16, 1), F32)
        e_bias = sbp("s_ebias", (128, 16), F32)
        rms_w = sbp("s_rmsw", (128, 1), F32)
        conv_w = {p: sbp(f"s_conv{p}", (128, 32), F32) for p in "kqv"}
        conv_wb = {p: sbp(f"s_convb{p}", (128, 8), F32) for p in "kqv"}
        gatewf = sbp("s_gatewf", (128, 8 * 16), F32)
        wbetaf = sbp("s_wbetaf", (128, 8 * 16), F32)
        wbetab = sbp("s_wbetab", (128, 8 * 16), BF16)
        W_st = [sbp(f"s_W{i}", (128, 512), F32) for i in range(2)]
        WA_st = [sbp(f"s_WA{i}", (128, 512), F32) for i in range(2)]
        carry = {p: sbp(f"s_carry{p}", (128, 24), BF16) for p in "kqv"}
        g_stg = sbp("s_gstg", (128, 1024), BF16)

        y_base = cur[0]
        y_t = sb("s_y", (128, HALF * NP), F32, y_base)
        g_t = sb("s_g", (128, HALF * NP), BF16, y_base)
        cur[0] = (tensors["s_y"][1] + 31) & ~31

        ab = cur[0]
        xn_half = sb("s_xnh", (128, HALF * NP), BF16, ab)
        gb = tensors["s_xnh"][1]
        comb_fb = sb("s_combf", (16, 4 * TA), BF16, gb)
        pa = gb + 4096
        xa_t = sb("s_xa", (128, TA * NP), F32, pa)
        xnf_t = sb("s_xnf", (128, TA * NP), F32, pa + 16384)
        xb_t = sb("s_xb", (128, TA * NP), BF16, pa + 32768)
        xsq_t = sb("s_xsq", (128, TA * NP), BF16, pa + 40960)
        g0 = pa + 49152
        stat_t = sb("s_stat", (1, 5 * TA), F32, g0)
        sc_f = sb("s_scf", (16, TA), F32, g0 + 12288)
        sc_tok = sb("s_sctok", (128, 64), F32, g0 + 14336)
        sel_t = sb("s_sel", (128, 16), F32, g0 + 14592)
        ge_t = sb("s_ge", (128, 32), F32, g0 + 14656)
        rank_t = sb("s_rank", (128, 16), F32, g0 + 14784)
        mask_t = sb("s_mask", (128, 16), F32, g0 + 14848)
        wsel_t = sb("s_wsel", (128, 16), F32, g0 + 14912)
        den_t = sb("s_den", (128, 4), F32, g0 + 14976)
        den_r = sb("s_denr", (128, 4), F32, g0 + 15008)
        comb_t = sb("s_comb", (128, 64), F32, g0 + 15040)
        comb_bs = sb("s_combbs", (128, 64), BF16, g0 + 15296)
        ewg = sb("s_ewg", (128, 8192), BF16, pa)
        h2_t = sb("s_h2", (128, TA * NP), BF16, pa + 16384)
        cb_s = sb("s_cb", (128, TA), BF16, pa + 24576)
        hs_s = sb("s_hs", (128, 2 * TA), BF16, pa + 25600)
        h1_s = sb("s_h1", (128, TA), BF16, pa + 27648)
        ewu = sb("s_ewu", (128, 8192), BF16, pa + 32768)
        ewd = sb("s_ewd", (128, 8192), BF16, pa + 49152)
        stg0 = sb("s_stg0", (128, 1024), F32, pa + 28672)
        stg1 = sb("s_stg1", (128, 1024), F32, pa + 65536)
        stg = (stg0, stg1)
        a_end = pa + 69632

        bcur = [y_base + 32768]

        def sbb(name, shape, dtype):
            t = sb(name, shape, dtype, bcur[0])
            bcur[0] = (tensors[name][1] + 31) & ~31
            return t

        wkb = sbb("s_wkb", (128, 8192), BF16)
        wqb = sbb("s_wqb", (128, 8192), BF16)
        wvb = sbb("s_wvb", (128, 8192), BF16)
        woutb = sbb("s_woutb", (128, 8192), BF16)
        xt_b = sbb("s_xtb", (128, TB * NP), F32)
        xnb = sbb("s_xnb", (128, TB * NP), BF16)
        xbb = sbb("s_xbb", (128, TB * NP), BF16)
        xsqb = sbb("s_xsqb", (128, TB * NP), BF16)
        statB = sbb("s_statB", (1, 5 * TB), F32)
        raw = sbb("s_raw", (128, 2 * (TB + 16)), BF16)
        pr_t = sbb("s_pr", (128, TB * NP), BF16)
        kf_t = sbb("s_kf", (128, TB * NP), BF16)
        qf_t = sbb("s_qf", (128, TB * NP), BF16)
        vf_t = sbb("s_vf", (128, TB * NP), BF16)
        bf_t = sbb("s_bf", (128, TB * NP), BF16)
        ac_t = sbb("s_ac", (128, 2 * TB), BF16)
        ksq_t = sbb("s_ksq", (128, 2 * TB), BF16)
        nrm2 = sbb("s_nrm2", (2, 4 * TB), F32)
        nrm2b = sbb("s_nrm2b", (2, 2 * TB), BF16)
        beta_f = sbb("s_betaf", (16, TB), BF16)
        beta_tok = sbb("s_btok", (128, (TB // 64) * 16), F32)
        NW = TB // 64
        ktok_t = sbb("s_ktok", (128, NW * 512), BF16)
        Rt_w = [sbb(f"s_R{w}", (128, 2 * 1024), BF16) for w in range(NW)]
        P_w = [sbb(f"s_P{w}", (128, 2 * 512), BF16) for w in range(NW)]
        Pt_w = [sbb(f"s_Pt{w}", (128, 2 * 512), BF16) for w in range(NW)]
        Tsb_w = [sbb(f"s_Tsb{w}", (128, 512), BF16) for w in range(NW)]
        Ttsb_w = [sbb(f"s_Ttsb{w}", (128, 512), BF16) for w in range(NW)]
        nTt_w = [sbb(f"s_nTt{w}", (128, 512), BF16) for w in range(NW)]
        Bpf_w = [sbb(f"s_Bpf{w}", (128, 512), BF16) for w in range(NW)]
        Kw_w = [sbb(f"s_Kw{w}", (128, 512), BF16) for w in range(NW)]
        Sm_w = [sbb(f"s_Sm{w}", (128, 512), BF16) for w in range(NW)]
        qa_w = [sbb(f"s_qa{w}", (128, 512), BF16) for w in range(NW)]
        qc_w = [sbb(f"s_qc{w}", (128, 512), BF16) for w in range(NW)]
        U_sb = sbb("s_U", (128, 512), BF16)
        o_store = sbb("s_ost", (128, (TB // 64) * 512), BF16)
        osq_sb = sb("s_osq", (128, 512), BF16, tensors["s_U"][0])
        osq_w1 = sb("s_osq1", (128, 512), BF16, tensors["s_qa0"][0])
        onrm1 = sb("s_onrm1", (2, 512), F32, tensors["s_R1"][0] + 2048)
        onrmb1 = sb("s_onrmb1", (2, 512), BF16, tensors["s_Sm1"][0])
        onrm = sb("s_onrm", (2, 512), F32, tensors["s_R0"][0] + 2048)
        onrmb = sb("s_onrmb", (2, 512), BF16, tensors["s_Sm0"][0])
        osc = sbb("s_osc", (128, (TB // 64) * 512), BF16)
        s_t = sbb("s_st", (128, 2 * TB), F32)
        ot_t = sbb("s_ot", (128, TB * NP), F32)
        W_bf = sbb("s_Wbf", (128, 512), BF16)
        WA_bf = sbb("s_WAbf", (128, 512), BF16)
        wa_tmp = sbb("s_watmp2", (128, 512), F32)
        b_end = bcur[0]
        assert max(a_end, b_end) <= 229376, (a_end, b_end)

        pb = [es.enter_context(nc.psum_tensor(f"pb{i}", [128, 512], F32))
              for i in range(6)]
        tb0 = es.enter_context(nc.psum_tensor("tb0", [128, 1024], BF16))
        tb1 = es.enter_context(nc.psum_tensor("tb1", [128, 1024], BF16))

        sL = Sem(nc, es, "sL")
        sS = Sem(nc, es, "sS")
        sT = Sem(nc, es, "sT")
        sV = Sem(nc, es, "sV")
        sC = Sem(nc, es, "sC")
        sP = Sem(nc, es, "sP")
        sems = {"T": sT, "V": sV, "C": sC, "P": sP, "S": sS, "L": sL}

        blk = es.enter_context(nc.Block())
        TE, VE, SE, PE_, SY = nc.tensor, nc.vector, nc.scalar, nc.gpsimd, nc.sync

        def wt(eng, key, n=None):
            sem = sems[key] if isinstance(key, str) else key
            n = sem.n if n is None else n
            assert n <= sem.n, (n, sem.n)
            if n > 0:
                eng.wait_ge(sem.h, n)

        def load(dst_ap, src_ap):
            SY.dma_start(out=dst_ap, in_=src_ap).then_inc(sL.h, 16)
            sL.n += 16
            return sL.n

        taps = {}

        def tap(name, sbuf_ap, shape, dtype, after):
            if name not in DBG or name in taps:
                return
            d = nc.dram_tensor("dbg_" + name, list(shape), dtype,
                               kind="ExternalOutput").ap()
            taps[name] = True
            wt(SY, after[0], after[1])
            SY.dma_start(out=d, in_=sbuf_ap).then_inc(sS.h, 16)
            sS.n += 16

        for nm, t in (("c_eye", c_eye), ("c_eyef", c_eyef), ("c_masks", c_masks),
                      ("c_tmat", c_tmat), ("c_a", c_a), ("c_w", c_w),
                      ("c_selpair", c_selpair), ("c_sele", c_sele),
                      ("c_ones1", c_ones1), ("c_ones1f", c_ones1f),
                      ("c_ones128", c_ones128), ("c_oneblk", c_oneblk),
                      ("c_bc2", c_bc2), ("c_eps", c_eps),
                      ("bk", b_k), ("bq", b_q), ("bv", b_v), ("bout", b_out),
                      ("bbeta", b_beta), ("ebias", e_bias), ("rmsw", rms_w),
                      ("convk", conv_w["k"]), ("convq", conv_w["q"]),
                      ("convv", conv_w["v"])):
            load(t[:], dt_in[nm])
        load(gatewf[:].rearrange("p (c e) -> p c e", c=8),
             dt_in["gatew"].rearrange("(c p) e -> p c e", p=128))
        load(wbetaf[:].rearrange("p (c e) -> p c e", c=8),
             dt_in["wbeta"].rearrange("(c p) e -> p c e", p=128))

        PE_.dma_start(out=out_d, in_=x_d).then_inc(sL.h, 16)
        sL.n += 16
        prefill_c = sL.n
        wt(PE_, "L")
        PE_.memset(W_st[0][:], 0.0)
        PE_.memset(WA_st[0][:], 0.0)
        for p in "kqv":
            PE_.memset(carry[p][:], 0.0)
        for p, bt in (("k", b_k), ("q", b_q), ("v", b_v)):
            PE_.tensor_tensor(
                out=conv_wb[p][:], in0=bt[:],
                in1=conv_w[p][:].rearrange("p (c t) -> p c t", c=8)[:, :, 3],
                op=ALU.mult)
        inst = PE_.tensor_copy(out=wbetab[:], in_=wbetaf[:])
        sP.inc(inst)

        bank_free = {}

        def acq(bank):
            w = bank_free.pop(bank.name, None)
            if w is not None:
                wt(TE, w[0], w[1])

        def rel(bank, key, count):
            bank_free[bank.name] = (key, count)

        def mm_acc(bank_ap, lhsT_list, rhs_list, inc_sem=None, tp=None):
            inst = None
            n = len(lhsT_list)
            for i in range(n):
                inst = TE.matmul(bank_ap, lhsT_list[i], rhs_list[i],
                                 start=(i == 0), stop=(i == n - 1),
                                 tile_position=tp)
            return inc_sem.inc(inst) if inc_sem is not None else inst

        def bcast8(ap2d, blocks=8, width=64):
            return bass.AP(ap2d.tensor, ap2d.offset,
                           [list(ap2d.ap[0]), [0, blocks], [1, width]])

        def cast_dma(dst_ap, src_ap, waits=()):
            for wk in waits:
                if wk is not None:
                    wt(PE_, wk[0], wk[1])
            PE_.dma_start(out=dst_ap, in_=src_ap).then_inc(sL.h, 16)
            sL.n += 16
            return sL.n

        stg_state = {"i": 0, "free": {}}

        def stream_cast(dst_ap, src_ap, nelem, ring, wait_c=None, eng="C"):
            slot = stg_state["i"] % 2
            stg_state["i"] += 1
            fr = stg_state["free"].get((ring[0].name, slot))
            if fr is not None:
                wt(SY, fr[0], fr[1])
            sslice = ring[slot][:, 0:nelem]
            load(sslice, src_ap)
            E = {"C": SE, "P": PE_}[eng]
            S = sems[eng]
            wt(E, "L")
            if wait_c is not None:
                wt(E, wait_c[0], wait_c[1])
            if eng == "C":
                inst = E.copy(out=dst_ap, in_=sslice)
            else:
                inst = E.tensor_copy(out=dst_ap, in_=sslice)
            cnt = S.inc(inst)
            stg_state["free"][(ring[0].name, slot)] = (eng, cnt)
            return cnt

        def emit_ln(xa, xbf, xsqf, stat, Tn, xnf_dst, xn_bf_dst, c_ld,
                    b_sx=None, b_sq=None, b_bc1=None, b_bc2=None):
            b_sx = b_sx if b_sx is not None else pb[0]
            b_sq = b_sq if b_sq is not None else pb[1]
            b_bc1 = b_bc1 if b_bc1 is not None else pb[2]
            b_bc2 = b_bc2 if b_bc2 is not None else pb[3]
            sq_off = Tn if b_sq.name == b_sx.name else 0
            bc_off = Tn if b_bc2.name == b_bc1.name else 0
            wt(SE, "L", c_ld)
            SE.copy(out=xbf[:], in_=xa[:])
            inst = SE.activation(out=xsqf[:], in_=xa[:], func=AF.Square)
            c_sq = sC.inc(inst)
            wt(TE, "C", c_sq)
            acq(b_sx)
            if sq_off == 0:
                acq(b_sq)
            x8 = xbf[:].rearrange("p (c t) -> p c t", c=8)
            q8 = xsqf[:].rearrange("p (c t) -> p c t", c=8)
            mm_acc(b_sx[0:1, 0:Tn], [c_ones128[:]] * 8,
                   [x8[:, c, :] for c in range(8)])
            t_st = mm_acc(b_sq[0:1, sq_off:sq_off + Tn], [c_ones128[:]] * 8,
                          [q8[:, c, :] for c in range(8)], sT)
            mean = stat[0:1, 0:Tn]
            varA = stat[0:1, Tn:2 * Tn]
            msq = stat[0:1, 2 * Tn:3 * Tn]
            sd = stat[0:1, 2 * Tn:3 * Tn]
            rstd = stat[0:1, 3 * Tn:4 * Tn]
            mrs = stat[0:1, 4 * Tn:5 * Tn]
            wt(VE, "T", t_st)
            VE.tensor_scalar(out=mean, in0=b_sx[0:1, 0:Tn], scalar1=1.0 / D,
                             scalar2=None, op0=ALU.mult)
            VE.tensor_scalar(out=varA, in0=b_sq[0:1, sq_off:sq_off + Tn],
                             scalar1=1.0 / D, scalar2=None, op0=ALU.mult)
            VE.tensor_mul(out=msq, in0=mean, in1=mean)
            VE.nop(cycle_cnt=96)
            inst = VE.tensor_sub(out=varA, in0=varA, in1=msq)
            v_var = sV.inc(inst)
            wt(SE, "V", v_var)
            SE.activation(out=sd, in_=varA, func=AF.Ln, bias=c_eps[0:1, 0:1])
            SE.nop(cycle_cnt=128)
            inst = SE.activation(out=rstd, in_=sd, func=AF.Exp, scale=-0.5)
            c_sd = sC.inc(inst)
            wt(VE, "C", c_sd)
            inst = VE.tensor_mul(out=mrs, in0=mean, in1=rstd)
            v_stats = sV.inc(inst)
            rel(b_sx, "V", v_stats)
            if sq_off == 0:
                rel(b_sq, "V", v_stats)
            wt(TE, "V", v_stats)
            acq(b_bc1)
            if bc_off == 0:
                acq(b_bc2)
            TE.matmul(b_bc1[:, 0:Tn], c_ones1f[:], rstd, start=True, stop=True)
            t_bc = sT.inc(TE.matmul(b_bc2[:, bc_off:bc_off + Tn], c_ones1f[:],
                                    mrs, start=True, stop=True))
            wt(VE, "T", t_bc)
            xa8 = xa[:].rearrange("p (c t) -> p c t", c=8)
            if xnf_dst is not None:
                f8 = xnf_dst[:].rearrange("p (c t) -> p c t", c=8)
                for c in range(8):
                    VE.tensor_mul(out=f8[:, c, :], in0=xa8[:, c, :],
                                  in1=b_bc1[:, 0:Tn])
                for c in range(8):
                    inst = VE.tensor_sub(out=f8[:, c, :], in0=f8[:, c, :],
                                         in1=b_bc2[:, bc_off:bc_off + Tn])
                v_xnf = sV.inc(inst)
                rel(b_bc1, "V", v_xnf)
                if bc_off == 0:
                    rel(b_bc2, "V", v_xnf)
                wt(PE_, "V", v_xnf)
                inst = PE_.tensor_copy(
                    out=xn_bf_dst,
                    in_=xnf_dst[:].rearrange("p (c t) -> p c t", c=8))
                return ("P", sP.inc(inst)), v_xnf
            else:
                b8 = xn_bf_dst[:].rearrange("p (c t) -> p c t", c=8)
                scr8 = xsqf[:].rearrange("p (c t) -> p c t", c=8)
                for c in range(8):
                    VE.tensor_mul(out=scr8[:, c, :], in0=xa8[:, c, :],
                                  in1=b_bc1[:, 0:Tn])
                for c in range(8):
                    inst = VE.tensor_sub(out=b8[:, c, :], in0=scr8[:, c, :],
                                         in1=b_bc2[:, bc_off:bc_off + Tn])
                v_xnf = sV.inc(inst)
                rel(b_bc1, "V", v_xnf)
                if bc_off == 0:
                    rel(b_bc2, "V", v_xnf)
                return ("V", v_xnf), v_xnf

        def phase_A(h, b_done):
            first = {"dma": b_done is not None, "y": b_done is not None,
                     "ms": b_done}
            prev_v = prev_c = 0
            for t in range(4):
                tok0 = h * HALF + t * TA
                if first["dma"]:
                    wt(SY, first["ms"][0], first["ms"][1])
                    first["dma"] = False
                if prev_v:
                    wt(SY, "V", prev_v)
                    wt(SY, "C", prev_c)
                    wt(VE, "P")
                c_ld = load(xa_t[:].rearrange("p (c t) -> p c t", c=8),
                            x_d.rearrange("(c p) t -> p c t", p=128)
                            [:, :, tok0:tok0 + TA])
                xn_dst = xn_half[:].rearrange("p (tt c t) -> p tt c t",
                                              tt=4, c=8)[:, t, :, :]
                (pk, pc), v_xnf = emit_ln(xa_t, xb_t, xsq_t, stat_t, TA,
                                          xnf_t, xn_dst, c_ld)
                prev_v, prev_c = v_xnf, sC.n
                acq(pb[4])
                g8 = gatewf[:].rearrange("p (c e) -> p c e", c=8)
                xf8 = xnf_t[:].rearrange("p (c t) -> p c t", c=8)
                wt(TE, "V", v_xnf)
                t_g = mm_acc(pb[4][0:16, 0:TA], [g8[:, c, :] for c in range(8)],
                             [xf8[:, c, :] for c in range(8)], sT)
                wt(SE, "T", t_g)
                inst = SE.activation(out=sc_f[:, 0:TA], in_=pb[4][0:16, 0:TA],
                                     func=AF.Sigmoid)
                c_sc = sC.inc(inst)
                rel(pb[4], "C", c_sc)
                acq(pb[5])
                wt(TE, "C", c_sc)
                insts = None
                for qq in range(4):
                    insts = TE.transpose(pb[5][0:128, qq * 16:(qq + 1) * 16],
                                         sc_f[:, qq * 128:(qq + 1) * 128],
                                         c_eyef[0:16, 0:16])
                t_tr = sT.inc(insts)
                wt(VE, "T", t_tr)
                inst = VE.tensor_copy(out=sc_tok[:], in_=pb[5][0:128, 0:64])
                v_sct = sV.inc(inst)
                rel(pb[5], "V", v_sct)
                for qq in range(4):
                    scq = sc_tok[:, qq * 16:(qq + 1) * 16]
                    inst = VE.tensor_add(out=sel_t[:], in0=scq, in1=e_bias[:])
                    nmid = sV.inc(inst)
                    wt(VE, "V", nmid)
                    inst = None
                    for e in range(16):
                        inst = VE.tensor_scalar(
                            out=ge_t[:, (e % 2) * 16:(e % 2) * 16 + 16],
                            in0=sel_t[:], scalar1=sel_t[:, e:e + 1],
                            scalar2=0.0, op0=ALU.is_le, op1=ALU.add,
                            accum_out=rank_t[:, e:e + 1])
                    nmid = sV.inc(inst)
                    wt(VE, "V", nmid)
                    VE.tensor_scalar(out=mask_t[:], in0=rank_t[:], scalar1=8.5,
                                     scalar2=None, op0=ALU.is_ge)
                    VE.nop(cycle_cnt=96)
                    inst = VE.scalar_tensor_tensor(out=wsel_t[:], in0=scq,
                                                   scalar=1.0,
                                                   in1=mask_t[:], op0=ALU.mult,
                                                   op1=ALU.mult,
                                                   accum_out=den_t[:, qq:qq + 1])
                    nmid = sV.inc(inst)
                    wt(VE, "V", nmid)
                    inst = VE.reciprocal(out=den_r[:, qq:qq + 1],
                                          in_=den_t[:, qq:qq + 1])
                    nmid = sV.inc(inst)
                    wt(VE, "V", nmid)
                    inst = VE.tensor_scalar(
                        out=comb_t[:, qq * 16:(qq + 1) * 16], in0=wsel_t[:],
                        scalar1=den_r[:, qq:qq + 1], scalar2=ROUTE_SCALE,
                        op0=ALU.mult, op1=ALU.mult)
                v_comb = sV.inc(inst)
                wt(PE_, "V", v_comb)
                inst = PE_.tensor_copy(out=comb_bs[:], in_=comb_t[:])
                p_cmb = sP.inc(inst)
                acq(tb0)
                wt(TE, "P", p_cmb)
                insts = None
                for qq in range(4):
                    insts = TE.transpose(tb0[0:16, qq * 128:(qq + 1) * 128],
                                         comb_bs[:, qq * 16:(qq + 1) * 16],
                                         c_eye[:])
                t_ctr = sT.inc(insts)
                wt(VE, "T", t_ctr)
                inst = VE.tensor_copy(out=comb_fb[:, t * TA:(t + 1) * TA],
                                      in_=tb0[0:16, 0:TA])
                v_cfb = sV.inc(inst)
                rel(tb0, "V", v_cfb)
                if t == 0 and h == 0:
                    tap("xn", xn_half[:, 0:4096], (128, 4096), BF16,
                        after=("P", sP.n))
                    tap("scores", sc_tok[:], (128, 64), F32, after=("V", sV.n))
                    tap("comb", comb_t[:], (128, 64), F32, after=("V", sV.n))

            gate_v = sV.n
            gate_p = sP.n

            last_gu_mm = 0
            last_wd_mm = 0
            last_h2 = 0
            hs_free = {0: 0, 1: 0}
            for ei in range(17):
                shared = (ei == 0)
                e = ei - 1
                srcs = (dt_in["sgw"], dt_in["suw"], dt_in["sdw"]) if shared \
                    else (dt_in["wg"][e], dt_in["wu"][e], dt_in["wd"][e])
                g_gate = ("V", gate_v) if ei == 0 else None
                wg_c = cast_dma(
                    ewg[:].rearrange("p (c i) -> p c i", c=8),
                    srcs[0].rearrange("(c p) i -> p c i", p=128),
                    (g_gate, ("T", last_gu_mm) if last_gu_mm else None))
                wu_c = cast_dma(
                    ewu[:].rearrange("p (c i) -> p c i", c=8),
                    srcs[1].rearrange("(c p) i -> p c i", p=128), ())
                wd_c = cast_dma(
                    ewd[:].rearrange("p (c i) -> p c i", c=8),
                    srcs[2].rearrange("(c p) i -> p c i", p=128),
                    (("T", last_wd_mm) if last_wd_mm else None,))

                xh8 = xn_half[:].rearrange("p (tt c t) -> p tt c t", tt=4, c=8)
                for t in range(4):
                    if not shared:
                        acq(pb[4])
                        wt(TE, "V", gate_v)
                        t_cb = sT.inc(TE.matmul(
                            pb[4][:, 0:TA], c_sele[:, e * 128:(e + 1) * 128],
                            comb_fb[:, t * TA:(t + 1) * TA],
                            start=True, stop=True))
                        wt(VE, "T", t_cb)
                        inst = VE.tensor_copy(out=cb_s[:], in_=pb[4][:, 0:TA])
                        v_cb = sV.inc(inst)
                        rel(pb[4], "V", v_cb)
                    h_last = None
                    for ic in range(8):
                        bg = pb[0 + 2 * (ic % 2)]
                        bu = pb[1 + 2 * (ic % 2)]
                        acq(bg); acq(bu)
                        wt(TE, "L", max(wg_c, wu_c))
                        wt(TE, "P", gate_p)
                        if t == 0 and ic == 0 and last_h2:
                            wt(TE, "V", last_h2)
                        g8 = ewg[:].rearrange("p (c i) -> p c i", c=8)
                        u8 = ewu[:].rearrange("p (c i) -> p c i", c=8)
                        mm_acc(bg[:, 0:TA],
                               [g8[:, c, ic * 128:(ic + 1) * 128]
                                for c in range(8)],
                               [xh8[:, t, c, :] for c in range(8)])
                        t_hu = mm_acc(
                            bu[:, 0:TA],
                            [u8[:, c, ic * 128:(ic + 1) * 128] for c in range(8)],
                            [xh8[:, t, c, :] for c in range(8)], sT)
                        if ic == 7 and t == 3:
                            last_gu_mm = t_hu
                        hslot = ic % 2
                        hss = hs_s[:, hslot * TA:(hslot + 1) * TA]
                        wt(SE, "T", t_hu)
                        if hs_free[hslot]:
                            wt(SE, "V", hs_free[hslot])
                        inst = SE.activation(out=hss, in_=bg[:, 0:TA],
                                             func=AF.Silu)
                        c_si = sC.inc(inst)
                        wt(VE, "C", c_si)
                        if shared:
                            inst = VE.tensor_mul(
                                out=h2_t[:, ic * TA:(ic + 1) * TA],
                                in0=hss, in1=bu[:, 0:TA])
                        else:
                            VE.tensor_mul(out=h1_s[:], in0=hss,
                                          in1=bu[:, 0:TA])
                            VE.nop(cycle_cnt=96)
                            inst = VE.tensor_mul(
                                out=h2_t[:, ic * TA:(ic + 1) * TA],
                                in0=h1_s[:], in1=cb_s[:])
                        h_last = sV.inc(inst)
                        hs_free[hslot] = h_last
                        rel(bg, "V", h_last); rel(bu, "V", h_last)
                    last_h2 = h_last
                    wd8 = ewd[:].rearrange("p (c i) -> p c i", c=8)
                    ys = y_t[:].rearrange("p (c tt t) -> p c tt t", c=8, tt=4)
                    for dc in range(8):
                        bd = pb[4 + (dc % 2)]
                        acq(bd)
                        wt(TE, "V", h_last)
                        wt(TE, "L", wd_c)
                        t_d = mm_acc(
                            bd[:, 0:TA],
                            [wd8[:, c, dc * 128:(dc + 1) * 128]
                             for c in range(8)],
                            [h2_t[:, c * TA:(c + 1) * TA] for c in range(8)], sT)
                        if t == 3 and dc == 7:
                            last_wd_mm = t_d
                        wt(VE, "T", t_d)
                        if first["y"]:
                            wt(VE, first["ms"][0], first["ms"][1])
                            first["y"] = False
                        if shared:
                            inst = VE.tensor_copy(out=ys[:, dc, t, :],
                                                  in_=bd[:, 0:TA])
                        else:
                            inst = VE.tensor_add(out=ys[:, dc, t, :],
                                                 in0=ys[:, dc, t, :],
                                                 in1=bd[:, 0:TA])
                        v_y = sV.inc(inst)
                        rel(bd, "V", v_y)
            y_done = sV.n
            wt(SE, "V", y_done)
            SE.activation(out=g_stg[:], in_=y_t[:, 0:1024], func=AF.Silu)
            inst = None
            for r in range(1, 16):
                inst = SE.activation(out=g_t[:, r * 1024:(r + 1) * 1024],
                                     in_=y_t[:, r * 1024:(r + 1) * 1024],
                                     func=AF.Silu)
            c_g0 = sC.inc(inst)
            wt(PE_, "C", c_g0)
            inst = PE_.tensor_copy(out=g_t[:, 0:1024], in_=g_stg[:])
            p_g = sP.inc(inst)
            if h == 0:
                tap("g", g_t[:, 0:4096], (128, 4096), BF16, after=("P", p_g))
            return ("P", p_g), ("T", last_wd_mm)

        def phase_B(h, g_ready, a_done):
            first_dma = True

            def gate_dma():
                nonlocal first_dma
                if first_dma:
                    wt(SY, a_done[0], a_done[1])
                    wt(SY, "V")
                    wt(SY, "P")
                    wt(SY, "C")
                    first_dma = False

            gate_dma()
            wcast = 0
            for mi, (wsrc, wdst) in enumerate(
                    ((dt_in["wk"], wkb), (dt_in["wq"], wqb),
                     (dt_in["wv"], wvb), (dt_in["wout"], woutb))):
                gts = (a_done, ("C", sC.n), ("V", sV.n)) if mi == 0 else ()
                wcast = cast_dma(
                    wdst[:].rearrange("p (c o) -> p c o", c=8),
                    wsrc.rearrange("(c p) o -> p c o", p=128), gts)
            chunk_i0 = h * (HALF // CH)
            wt(PE_, "C")
            wt(PE_, "V")
            PE_.tensor_copy(out=W_bf[:], in_=W_st[0][:])
            inst = PE_.tensor_copy(out=WA_bf[:], in_=WA_st[0][:])
            wbf0 = sP.inc(inst)

            NSUB = HALF // TB
            sh = {"slot_free": {0: 0, 1: 0}, "ksq_free": {0: 0, 1: 0},
                  "vsil_free": {0: 0, 1: 0}, "wfree": {}, "kqvb": 0,
                  "ln_c": 0, "ln_v": 0, "ot_free": 0, "prep": {},
                  "wbf_c": wbf0}
            pslices = [slice(0, 64), slice(64, 128)]
            tps = [(0, 0), (64, 64)]
            p3 = lambda ap: ap.rearrange("q (p z) -> q p z", z=64)

            def pmm(bank, lh, rh, p, par, ss=(True, True)):
                return TE.matmul(
                    bank[pslices[par], p * 64:p * 64 + 64], lh, rh,
                    start=ss[0], stop=ss[1], tile_position=tps[par])

            kf_s = lambda par, p, w: kf_t[pslices[par],
                                          p * TB + w * 64:p * TB + w * 64 + 64]
            qf_s = lambda par, p, w: qf_t[pslices[par],
                                          p * TB + w * 64:p * TB + w * 64 + 64]
            bf_s = lambda par, p, w: bf_t[pslices[par],
                                          p * TB + w * 64:p * TB + w * 64 + 64]

            def front(s_i):
                tok0 = h * HALF + s_i * TB
                if sh["ln_v"]:
                    wt(SY, "V", sh["ln_v"])
                    wt(SY, "C", sh["ln_c"])
                c_ld = load(xt_b[:].rearrange("p (c t) -> p c t", c=8),
                            x_d.rearrange("(c p) t -> p c t", p=128)
                            [:, :, tok0:tok0 + TB])
                (ek, ec), v_xnf = emit_ln(xt_b, xbb, xsqb, statB, TB, None,
                                          xnb, c_ld, pb[4], pb[4], pb[5], pb[5])
                sh["ln_v"] = v_xnf
                sh["ln_c"] = sC.n
                xn8 = xnb[:].rearrange("p (c t) -> p c t", c=8)
                wt(TE, ek, ec)
                wt(TE, "L", wcast)
                yield "F0"

                acq(pb[4])
                wb8 = wbetab[:].rearrange("p (c e) -> p c e", c=8)
                t_b = mm_acc(pb[4][0:16, 0:TB],
                             [wb8[:, c, :] for c in range(8)],
                             [xn8[:, c, :] for c in range(8)], sT)
                wt(SE, "T", t_b)
                inst = SE.activation(out=beta_f[:], in_=pb[4][0:16, 0:TB],
                                     func=AF.Sigmoid, bias=b_beta[:])
                c_bf = sC.inc(inst)
                rel(pb[4], "C", c_bf)
                acq(tb0)
                wt(TE, "C", c_bf)
                insts = None
                for w in range(TB // 64):
                    TE.transpose(tb0[0:64, w * 16:(w + 1) * 16],
                                 beta_f[:, w * 64:(w + 1) * 64],
                                 c_eye[0:16, 0:16], tile_position=(0, 0))
                    insts = TE.transpose(
                        tb0[64:128, 256 + w * 16:256 + (w + 1) * 16],
                        beta_f[:, w * 64:(w + 1) * 64],
                        c_eye[0:16, 0:16], tile_position=(0, 64))
                t_bt = sT.inc(insts)
                wt(VE, "T", t_bt)
                nbt = (TB // 64) * 16
                VE.tensor_copy(out=beta_tok[0:64, :], in_=tb0[0:64, 0:nbt])
                inst = VE.tensor_copy(out=beta_tok[64:128, :],
                                      in_=tb0[64:128, 256:256 + nbt])
                v_bt = sV.inc(inst)
                rel(tb0, "V", v_bt)
                sh["prep"]["v_bt"] = v_bt
                yield "F1"

                v_cv = {}
                pr_cfree = 0
                for pname, wmat, bias_t in (("k", wkb, b_k), ("q", wqb, b_q),
                                            ("v", wvb, b_v)):
                    if pr_cfree:
                        wt(VE, "C", pr_cfree)
                    w8 = wmat[:].rearrange("p (c o) -> p c o", c=8)
                    for oc in range(8):
                        bk_ = pb[4 + oc % 2]
                        acq(bk_)
                        t_p = mm_acc(
                            bk_[:, 0:TB],
                            [w8[:, c, oc * 128:(oc + 1) * 128]
                             for c in range(8)],
                            [xn8[:, c, :] for c in range(8)], sT)
                        slot = oc % 2
                        rs = raw[:, slot * (TB + 16):slot * (TB + 16) + TB + 16]
                        if sh["slot_free"][slot]:
                            wt(PE_, "V", sh["slot_free"][slot])
                        inst = PE_.tensor_copy(
                            out=rs[:, 0:3],
                            in_=carry[pname][:, oc * 3:oc * 3 + 3])
                        p_h = sP.inc(inst)
                        wt(VE, "T", t_p)
                        VE.tensor_scalar(out=rs[:, 3:3 + TB], in0=bk_[:, 0:TB],
                                         scalar1=bias_t[:, oc:oc + 1],
                                         scalar2=None, op0=ALU.add)
                        inst = VE.tensor_scalar(
                            out=ac_t[:, slot * TB:(slot + 1) * TB],
                            in0=bk_[:, 0:TB],
                            scalar1=conv_w[pname][:, oc * 4 + 3:oc * 4 + 4],
                            scalar2=conv_wb[pname][:, oc:oc + 1],
                            op0=ALU.mult, op1=ALU.add)
                        v_ev = sV.inc(inst)
                        rel(bk_, "V", v_ev)
                        wt(VE, "P", p_h)
                        cwp = conv_w[pname]
                        dst = pr_t[:, oc * TB:(oc + 1) * TB]
                        VE.scalar_tensor_tensor(
                            out=ac_t[:, slot * TB:(slot + 1) * TB],
                            in0=rs[:, 2:2 + TB],
                            scalar=cwp[:, oc * 4 + 2:oc * 4 + 3],
                            in1=ac_t[:, slot * TB:(slot + 1) * TB],
                            op0=ALU.mult, op1=ALU.add)
                        VE.nop(cycle_cnt=96)
                        VE.scalar_tensor_tensor(
                            out=ac_t[:, slot * TB:(slot + 1) * TB],
                            in0=rs[:, 1:1 + TB],
                            scalar=cwp[:, oc * 4 + 1:oc * 4 + 2],
                            in1=ac_t[:, slot * TB:(slot + 1) * TB],
                            op0=ALU.mult, op1=ALU.add)
                        VE.nop(cycle_cnt=96)
                        inst = VE.scalar_tensor_tensor(
                            out=dst, in0=rs[:, 0:TB],
                            scalar=cwp[:, oc * 4 + 0:oc * 4 + 1],
                            in1=ac_t[:, slot * TB:(slot + 1) * TB],
                            op0=ALU.mult, op1=ALU.add)
                        v_cv[(pname, oc)] = sV.inc(inst)
                        sh["slot_free"][slot] = v_cv[(pname, oc)]
                        wt(PE_, "V", v_ev)
                        inst = PE_.tensor_copy(
                            out=carry[pname][:, oc * 3:oc * 3 + 3],
                            in_=rs[:, TB:TB + 3])
                        sP.inc(inst)
                    if pname == "k":
                        yield "F2"
                        for op in range(0, 8, 2):
                            nrm_pair(pr_t, kf_t, v_cv, "k", op)
                        pr_cfree = sC.n
                        yield "F2n"
                    elif pname == "q":
                        yield "F3"
                        for op in range(0, 8, 2):
                            nrm_pair(pr_t, qf_t, v_cv, "q", op)
                        pr_cfree = sC.n
                        yield "F3n"
                    else:
                        yield "F4"
                        for oc in range(8):
                            ks = ksq_t[:, (oc % 2) * TB:(oc % 2 + 1) * TB]
                            acq(pb[5])
                            wt(TE, "V", sh["prep"]["v_bt"])
                            t_bb = sT.inc(TE.matmul(
                                pb[5][:, 0:TB],
                                c_selpair[:, oc * 128:(oc + 1) * 128],
                                beta_f[:], start=True, stop=True))
                            wt(SE, "V", v_cv[("v", oc)])
                            if sh["vsil_free"][oc % 2]:
                                wt(SE, "V", sh["vsil_free"][oc % 2])
                            inst = SE.activation(
                                out=ks, in_=pr_t[:, oc * TB:(oc + 1) * TB],
                                func=AF.Silu)
                            c_sv = sC.inc(inst)
                            wt(VE, "T", t_bb)
                            wt(VE, "C", c_sv)
                            if oc == 0:
                                wt(VE, "T", sh["kqvb"])
                            VE.tensor_mul(out=vf_t[:, oc * TB:(oc + 1) * TB],
                                          in0=ks, in1=pb[5][:, 0:TB])
                            inst = VE.tensor_mul(
                                out=bf_t[:, oc * TB:(oc + 1) * TB],
                                in0=kf_t[:, oc * TB:(oc + 1) * TB],
                                in1=pb[5][:, 0:TB])
                            v_vb = sV.inc(inst)
                            sh["vsil_free"][oc % 2] = v_vb
                            rel(pb[5], "V", v_vb)
                        yield "F4n"
                sh["prep"]["v_prep"] = sV.n
                sh["prep"]["ready"] = True

            def nrm_pair(src_t, dst_t, v_cv, pname, op):
                acq(pb[4])
                t_ns = None
                for j in range(2):
                    wt(SE, "V", v_cv[(pname, op + j)])
                    if sh["ksq_free"][j]:
                        wt(SE, "T", sh["ksq_free"][j])
                    inst = SE.activation(
                        out=ksq_t[:, j * TB:(j + 1) * TB],
                        in_=src_t[:, (op + j) * TB:(op + j + 1) * TB],
                        func=AF.Square)
                    c_k2 = sC.inc(inst)
                    wt(TE, "C", c_k2)
                    t_ns = TE.matmul(pb[4][0:2, j * TB:(j + 1) * TB],
                                     c_oneblk[:],
                                     ksq_t[:, j * TB:(j + 1) * TB],
                                     start=True, stop=True)
                t_ns = sT.inc(t_ns)
                sh["ksq_free"][0] = sh["ksq_free"][1] = t_ns
                wt(SE, "T", t_ns)
                SE.activation(out=nrm2[:, 0:2 * TB], in_=pb[4][0:2, 0:2 * TB],
                              func=AF.Sqrt)
                SE.nop(cycle_cnt=128)
                SE.activation(out=nrm2[:, 2 * TB:4 * TB],
                              in_=nrm2[:, 0:2 * TB],
                              func=AF.Ln, bias=c_eps[0:2, 1:2])
                SE.nop(cycle_cnt=128)
                inst = SE.activation(out=nrm2b[:, 0:2 * TB],
                                     in_=nrm2[:, 2 * TB:4 * TB],
                                     func=AF.Exp, scale=-1.0)
                c_nr = sC.inc(inst)
                rel(pb[4], "C", c_nr)
                acq(pb[5])
                wt(TE, "C", c_nr)
                t_b2 = None
                for j in range(2):
                    t_b2 = TE.matmul(pb[5][:, j * TB:(j + 1) * TB], c_bc2[:],
                                     nrm2b[0:2, j * TB:(j + 1) * TB],
                                     start=True, stop=True)
                t_b2 = sT.inc(t_b2)
                wt(VE, "T", t_b2)
                if op == 0:
                    wt(VE, "T", sh["kqvb"])
                inst = None
                for j in range(2):
                    inst = VE.tensor_mul(
                        out=dst_t[:, (op + j) * TB:(op + j + 1) * TB],
                        in0=src_t[:, (op + j) * TB:(op + j + 1) * TB],
                        in1=pb[5][:, j * TB:(j + 1) * TB])
                v_nf = sV.inc(inst)
                rel(pb[5], "V", v_nf)

            def back(s_i):
                v_prep = sh["prep"]["v_prep"]
                stw = [dict() for _ in range(NW)]
                for w in range(NW):
                    tbK = tb0 if w % 2 == 0 else tb1
                    acq(tbK)
                    wt(TE, "V", v_prep)
                    wt(TE, "T", sh["wfree"].get(w, 0))
                    i0_ = None
                    for p in range(8):
                        sl = slice(p * TB + w * 64, p * TB + w * 64 + 64)
                        TE.transpose(tbK[0:64, p * 64:p * 64 + 64],
                                     kf_t[0:64, sl], c_eye[0:64, 0:64],
                                     tile_position=(0, 0))
                        TE.transpose(tbK[64:128, 512 + p * 64:512 + p * 64 + 64],
                                     kf_t[64:128, sl], c_eye[64:128, 64:128],
                                     tile_position=(64, 64))
                    for p in range(8):
                        sl = slice(p * TB + w * 64, p * TB + w * 64 + 64)
                        TE.transpose(tbK[0:64, 512 + p * 64:512 + p * 64 + 64],
                                     vf_t[0:64, sl], c_eye[0:64, 0:64],
                                     tile_position=(0, 0))
                        i0_ = TE.transpose(tbK[64:128, p * 64:p * 64 + 64],
                                           vf_t[64:128, sl],
                                           c_eye[64:128, 64:128],
                                           tile_position=(64, 64))
                    stw[w]["t_tp"] = sT.inc(i0_)
                for w in range(NW):
                    tbK = tb0 if w % 2 == 0 else tb1
                    ci = chunk_i0 + s_i * NW + w
                    stw[w]["ci"] = ci
                    wt(VE, "T", stw[w]["t_tp"])
                    VE.tensor_copy(out=ktok_t[0:64, w * 512:(w + 1) * 512],
                                   in_=tbK[0:64, 0:512])
                    VE.tensor_copy(out=ktok_t[64:128, w * 512:(w + 1) * 512],
                                   in_=tbK[64:128, 512:1024])
                    r0 = Rt_w[w][:, 0:1024]
                    r0v = r0.rearrange("q (p z) -> q p z", z=128)
                    VE.tensor_copy(
                        out=r0v[0:64, :, 0:64],
                        in_=tbK[0:64, 512:1024].rearrange(
                            "q (p z) -> q p z", z=64))
                    inst = VE.tensor_copy(
                        out=r0v[64:128, :, 0:64],
                        in_=tbK[64:128, 0:512].rearrange(
                            "q (p z) -> q p z", z=64))
                    v_tp = sV.inc(inst)
                    rel(tbK, "V", v_tp)
                    stw[w]["r0v"] = r0v
                    for par in range(2):
                        pr = pslices[par]
                        bt0 = beta_tok[pr, w * 16 + par:w * 16 + par + 1]
                        btb = bass.AP(bt0.tensor, bt0.offset,
                                      [list(bt0.ap[0]), [2, 8], [0, 64]])
                        kt3 = ktok_t[pr, w * 512:(w + 1) * 512].rearrange(
                            "q (p z) -> q p z", z=64)
                        inst = VE.tensor_mul(out=r0v[pr, :, 64:128], in0=kt3,
                                             in1=btb)
                    VE.nop(cycle_cnt=96)
                    inst = VE.tensor_scalar(
                        out=Kw_w[w][:], in0=ktok_t[:, w * 512:(w + 1) * 512],
                        scalar1=c_w[:, ci:ci + 1], scalar2=None, op0=ALU.mult)
                    stw[w]["v_r0"] = sV.inc(inst)
                yield "B0"

                for w in range(NW):
                    bT = pb[0] if w % 2 == 0 else pb[2]
                    bTt = pb[1] if w % 2 == 0 else pb[3]
                    acq(bT); acq(bTt)
                    wt(TE, "V", stw[w]["v_r0"])
                    i1_ = None
                    for p in range(8):
                        for par in range(2):
                            pmm(bT, bf_s(par, p, w), kf_s(par, p, w), p, par)
                            i1_ = pmm(bTt, kf_s(par, p, w), bf_s(par, p, w),
                                      p, par)
                    t_T = sT.inc(i1_)
                    wt(VE, "T", t_T)
                    VE.tensor_mul(out=p3(Tsb_w[w][:]), in0=p3(bT[:, :]),
                                  in1=bcast8(c_masks[:, 0:64]))
                    inst = VE.tensor_mul(out=p3(Ttsb_w[w][:]),
                                         in0=p3(bTt[:, :]),
                                         in1=bcast8(c_masks[:, 64:128]))
                    VE.nop(cycle_cnt=96)
                    inst = VE.tensor_scalar(out=nTt_w[w][:], in0=Ttsb_w[w][:],
                                            scalar1=-1.0, scalar2=None,
                                            op0=ALU.mult)
                    v_T = sV.inc(inst)
                    rel(bT, "V", v_T); rel(bTt, "V", v_T)
                    stw[w]["v_sq"] = v_T
                    stw[w]["c_sq"] = 0
                    stw[w]["prevP"] = Tsb_w[w]
                    stw[w]["prevPt"] = Ttsb_w[w]
                    stw[w]["rcur"] = 0
                yield "B1"

                def emit_apply(w, lhsT_tile):
                    S = stw[w]
                    rcur = S["rcur"]
                    rnxt = 1 - rcur
                    if w % 2 == 0:
                        b0a, b0b = pb[4], pb[5]
                        b1a, b1b = pb[0], pb[1]
                    else:
                        b0a, b0b = pb[2], pb[3]
                        b1a, b1b = pb[4], pb[5]
                    wt(TE, "V", max(S["v_sq"], S.get("v_ap", 0)))
                    if S.get("c_ap") or S["c_sq"]:
                        wt(TE, "C", max(S.get("c_ap", 0), S["c_sq"]))
                    i4_ = None
                    for par in range(2):
                        ba, bb_ = (b0a, b0b) if par == 0 else (b1a, b1b)
                        acq(ba); acq(bb_)
                        for p in range(8):
                            bank = ba if p < 4 else bb_
                            reg = bank[pslices[par],
                                       (p % 4) * 128:(p % 4) * 128 + 128]
                            rr = Rt_w[w][pslices[par],
                                         rcur * 1024 + p * 128:
                                         rcur * 1024 + p * 128 + 128]
                            TE.matmul(reg, lhsT_tile[pslices[par],
                                                     p * 64:p * 64 + 64],
                                      rr, start=True, stop=False,
                                      tile_position=tps[par])
                            i4_ = TE.matmul(
                                reg, c_eye[pslices[par], pslices[par]],
                                rr, start=False, stop=True,
                                tile_position=tps[par])
                    t_ap = sT.inc(i4_)
                    wt(SE, "T", t_ap)
                    SE.copy(out=Rt_w[w][0:64, rnxt * 1024:rnxt * 1024 + 512],
                            in_=b0a[0:64, :])
                    inst = SE.copy(
                        out=Rt_w[w][0:64, rnxt * 1024 + 512:rnxt * 1024 + 1024],
                        in_=b0b[0:64, :])
                    c_ap = sC.inc(inst)
                    wt(VE, "T", t_ap)
                    VE.tensor_copy(
                        out=Rt_w[w][64:128, rnxt * 1024:rnxt * 1024 + 512],
                        in_=b1a[64:128, :])
                    inst = VE.tensor_copy(
                        out=Rt_w[w][64:128,
                                    rnxt * 1024 + 512:rnxt * 1024 + 1024],
                        in_=b1b[64:128, :])
                    v_ap = sV.inc(inst)
                    rel(b0a, "C", c_ap); rel(b0b, "C", c_ap)
                    rel(b1a, "V", v_ap); rel(b1b, "V", v_ap)
                    S["c_ap"] = c_ap
                    S["v_ap"] = v_ap
                    S["rcur"] = rnxt

                def emit_sq(w, j):
                    S = stw[w]
                    bP = pb[0] if w % 2 == 0 else pb[2]
                    bPt = pb[1] if w % 2 == 0 else pb[3]
                    acq(bP); acq(bPt)
                    wt(TE, "V", S["v_sq"])
                    if S["c_sq"]:
                        wt(TE, "C", S["c_sq"])
                    dstP = P_w[w][:, (j % 2) * 512:(j % 2) * 512 + 512]
                    dstPt = Pt_w[w][:, (j % 2) * 512:(j % 2) * 512 + 512]
                    i3_ = None
                    for p in range(8):
                        for par in range(2):
                            pmm(bP, S["prevPt"][pslices[par],
                                                p * 64:p * 64 + 64],
                                S["prevP"][pslices[par], p * 64:p * 64 + 64],
                                p, par)
                            i3_ = pmm(bPt, S["prevP"][pslices[par],
                                                      p * 64:p * 64 + 64],
                                      S["prevPt"][pslices[par],
                                                  p * 64:p * 64 + 64],
                                      p, par)
                    t_sq = sT.inc(i3_)
                    wt(SE, "T", t_sq)
                    inst = SE.copy(out=dstP, in_=bP[:, :])
                    S["c_sq"] = sC.inc(inst)
                    wt(VE, "T", t_sq)
                    inst = VE.tensor_copy(out=dstPt, in_=bPt[:, :])
                    S["v_sq"] = sV.inc(inst)
                    rel(bP, "C", S["c_sq"]); rel(bPt, "V", S["v_sq"])
                    S["prevP"] = dstP
                    S["prevPt"] = dstPt

                emit_apply(0, nTt_w[0])
                if NW > 1:
                    emit_apply(1, nTt_w[1])
                yield "B2"
                for j in range(1, 6):
                    for w in range(NW):
                        emit_sq(w, j)
                        emit_apply(w, stw[w]["prevPt"])
                    if j in (2, 4):
                        yield f"B{j + 1}"
                for w in range(NW):
                    assert stw[w]["rcur"] == 0
                yield "B7"

                for w in range(NW):
                    S = stw[w]
                    tbK = tb0 if w % 2 == 0 else tb1
                    acq(tbK)
                    wt(TE, "C", S["c_ap"])
                    wt(TE, "V", S["v_ap"])
                    rfv = Rt_w[w][:, 0:1024].rearrange("q (p z) -> q p z",
                                                       z=128)
                    S["rfv"] = rfv
                    i5_ = None
                    for p in range(8):
                        TE.transpose(tbK[0:64, p * 64:p * 64 + 64],
                                     rfv[0:64, p, 64:128], c_eye[0:64, 0:64],
                                     tile_position=(0, 0))
                        i5_ = TE.transpose(
                            tbK[64:128, 512 + p * 64:512 + p * 64 + 64],
                            rfv[64:128, p, 64:128], c_eye[64:128, 64:128],
                            tile_position=(64, 64))
                    t_bp = sT.inc(i5_)
                    wt(VE, "T", t_bp)
                    VE.tensor_copy(out=Bpf_w[w][0:64, :], in_=tbK[0:64, 0:512])
                    inst = VE.tensor_copy(out=Bpf_w[w][64:128, :],
                                          in_=tbK[64:128, 512:1024])
                    v_bp = sV.inc(inst)
                    rel(tbK, "V", v_bp)
                    S["v_bp"] = v_bp
                    bS = pb[2] if w % 2 == 0 else pb[3]
                    acq(bS)
                    i6_ = None
                    for p in range(8):
                        for par in range(2):
                            i6_ = pmm(bS, kf_s(par, p, w), qf_s(par, p, w),
                                      p, par)
                    t_S = sT.inc(i6_)
                    ci = S["ci"]
                    wt(VE, "T", t_S)
                    inst = VE.tensor_mul(
                        out=p3(Sm_w[w][:]), in0=p3(bS[:, :]),
                        in1=bcast8(c_tmat[:, ci * 64:ci * 64 + 64]))
                    v_S = sV.inc(inst)
                    rel(bS, "V", v_S)
                    qfw3 = qf_t[:].rearrange("q (p t) -> q p t",
                                             t=TB)[:, :, w * 64:w * 64 + 64]
                    VE.tensor_mul(out=p3(qa_w[w][:]), in0=qfw3,
                                  in1=bcast8(c_a[:, ci * 64:ci * 64 + 64]))
                    VE.nop(cycle_cnt=96)
                    inst = VE.tensor_sub(out=p3(qc_w[w][:]), in0=qfw3,
                                         in1=p3(qa_w[w][:]))
                    S["v_q"] = sV.inc(inst)
                sh["kqvb"] = sT.n
                yield "B8"

                c_O = 0
                for w in range(NW):
                    S = stw[w]
                    ci = S["ci"]
                    cf1 = float(coef1[ci])
                    cf2 = float(coef2[ci])
                    rfv = S["rfv"]
                    acq(pb[3])
                    wt(TE, "V", S["v_bp"])
                    wt(TE, "P", sh.get("wbf_c", sP.n))
                    i7_ = None
                    for p in range(8):
                        for par in range(2):
                            i7_ = pmm(pb[3], Bpf_w[w][pslices[par],
                                                      p * 64:p * 64 + 64],
                                      W_bf[pslices[par], p * 64:p * 64 + 64],
                                      p, par)
                    t_U = sT.inc(i7_)
                    wt(VE, "T", t_U)
                    inst = VE.tensor_sub(
                        out=U_sb[:].rearrange("q (p z) -> q p z", z=64),
                        in0=rfv[:, :, 0:64],
                        in1=pb[3][:, :].rearrange("q (p z) -> q p z", z=64))
                    v_U = sV.inc(inst)
                    rel(pb[3], "V", v_U)

                    acq(pb[0]); acq(pb[1])
                    wt(TE, "V", max(v_U, S["v_q"]))
                    i8_ = None
                    for p in range(8):
                        for par in range(2):
                            bo = pb[0] if par == 0 else pb[1]
                            pmm(bo, WA_bf[pslices[par], p * 64:p * 64 + 64],
                                qa_w[w][pslices[par], p * 64:p * 64 + 64],
                                p, par, ss=(True, False))
                            pmm(bo, W_bf[pslices[par], p * 64:p * 64 + 64],
                                qc_w[w][pslices[par], p * 64:p * 64 + 64],
                                p, par, ss=(False, False))
                            i8_ = pmm(bo, U_sb[pslices[par],
                                              p * 64:p * 64 + 64],
                                      Sm_w[w][pslices[par],
                                              p * 64:p * 64 + 64],
                                      p, par, ss=(False, True))
                    t_O = sT.inc(i8_)

                    acq(pb[2]); acq(pb[3])
                    i9_ = None
                    for p in range(8):
                        for par in range(2):
                            pmm(pb[2], ktok_t[pslices[par],
                                              w * 512 + p * 64:
                                              w * 512 + p * 64 + 64],
                                U_sb[pslices[par], p * 64:p * 64 + 64], p, par)
                            i9_ = pmm(pb[3], Kw_w[w][pslices[par],
                                                     p * 64:p * 64 + 64],
                                      U_sb[pslices[par], p * 64:p * 64 + 64],
                                      p, par)
                    t_W = sT.inc(i9_)
                    sh["wfree"][w] = t_W

                    cur_i, nxt_i = ci % 2, (ci + 1) % 2
                    wt(VE, "T", t_W)
                    VE.scalar_tensor_tensor(out=wa_tmp[:], in0=W_st[cur_i][:],
                                            scalar=cf2, in1=pb[3][:, :],
                                            op0=ALU.mult, op1=ALU.add)
                    VE.tensor_add(out=W_st[nxt_i][:], in0=W_st[cur_i][:],
                                  in1=pb[2][:, :])
                    inst = VE.scalar_tensor_tensor(out=WA_st[nxt_i][:],
                                                   in0=WA_st[cur_i][:],
                                                   scalar=cf1, in1=wa_tmp[:],
                                                   op0=ALU.mult, op1=ALU.add)
                    v_st = sV.inc(inst)
                    rel(pb[2], "V", v_st); rel(pb[3], "V", v_st)
                    wt(PE_, "V", v_st)
                    PE_.tensor_copy(out=W_bf[:], in_=W_st[nxt_i][:])
                    inst = PE_.tensor_copy(out=WA_bf[:], in_=WA_st[nxt_i][:])
                    sh["wbf_c"] = sP.inc(inst)

                    wt(SE, "T", t_O)
                    SE.copy(out=o_store[0:64, w * 512:(w + 1) * 512],
                            in_=pb[0][0:64, :])
                    inst = SE.copy(out=o_store[64:128, w * 512:(w + 1) * 512],
                                   in_=pb[1][64:128, :])
                    c_O = sC.inc(inst)
                    rel(pb[0], "C", c_O); rel(pb[1], "C", c_O)
                    yield f"B9.{w}"

                rms_banks = {0: (pb[2], pb[3]), 1: (pb[0], pb[1])}
                osq2 = {0: osq_sb, 1: osq_w1}
                onr2 = {0: onrm, 1: onrm1}
                onb2 = {0: onrmb, 1: onrmb1}
                WL = list(range(TB // 64))
                cq = {}
                tn_ = {}
                cn_ = {}
                tb_ = {}
                wt(SE, "T", sh["wfree"].get(NW - 1, 0))
                for w in WL:
                    wt(SE, "C", c_O)
                    if sh.get("rms_free"):
                        wt(SE, "V", sh["rms_free"])
                    inst = SE.activation(out=osq2[w][:],
                                         in_=o_store[:, w * 512:(w + 1) * 512],
                                         func=AF.Square)
                    cq[w] = sC.inc(inst)
                for w in WL:
                    bs, bb_ = rms_banks[w]
                    acq(bs)
                    wt(TE, "C", cq[w])
                    i10 = None
                    for p in range(8):
                        i10 = TE.matmul(bs[0:2, p * 64:p * 64 + 64],
                                        c_oneblk[:],
                                        osq2[w][:, p * 64:p * 64 + 64],
                                        start=True, stop=True)
                    tn_[w] = sT.inc(i10)
                for w in WL:
                    bs, bb_ = rms_banks[w]
                    wt(SE, "T", tn_[w])
                    SE.activation(out=onr2[w][:, 0:512], in_=bs[0:2, 0:512],
                                  func=AF.Ln, scale=1.0 / DH,
                                  bias=c_eps[0:2, 1:2])
                    SE.nop(cycle_cnt=128)
                    inst = SE.activation(out=onb2[w][:, 0:512],
                                         in_=onr2[w][:, 0:512],
                                         func=AF.Exp, scale=-0.5)
                    cn_[w] = sC.inc(inst)
                    rel(bs, "C", cn_[w])
                for w in WL:
                    bs, bb_ = rms_banks[w]
                    acq(bb_)
                    wt(TE, "C", cn_[w])
                    i11 = None
                    for p in range(8):
                        i11 = TE.matmul(bb_[:, p * 64:p * 64 + 64], c_bc2[:],
                                        onb2[w][0:2, p * 64:p * 64 + 64],
                                        start=True, stop=True)
                    tb_[w] = sT.inc(i11)
                v_os = 0
                for w in WL:
                    bs, bb_ = rms_banks[w]
                    wt(VE, "T", tb_[w])
                    VE.tensor_mul(out=osq2[w][:],
                                  in0=o_store[:, w * 512:(w + 1) * 512],
                                  in1=bb_[:, :])
                    VE.nop(cycle_cnt=96)
                    inst = VE.tensor_scalar(out=osc[:, w * 512:(w + 1) * 512],
                                            in0=osq2[w][:],
                                            scalar1=rms_w[:, 0:1],
                                            scalar2=None, op0=ALU.mult)
                    v_os = sV.inc(inst)
                    rel(bb_, "V", v_os)
                sh["rms_free"] = v_os
                yield "B11"

                wo8 = woutb[:].rearrange("p (c o) -> p c o", c=8)
                oscv = osc[:].rearrange("q (w2 p c2) -> q p w2 c2",
                                        w2=TB // 64, c2=64)
                v_out = 0
                v_osl = sV.n
                tok0 = h * HALF + s_i * TB
                for oc in range(8):
                    bk_ = pb[oc % 2]
                    acq(bk_)
                    wt(TE, "V", v_osl)
                    t_s = mm_acc(
                        bk_[:, 0:TB],
                        [wo8[:, c, oc * 128:(oc + 1) * 128] for c in range(8)],
                        [oscv[:, c, :, :] for c in range(8)], sT)
                    slot = oc % 2
                    wt(VE, "T", t_s)
                    wt(VE, g_ready[0], g_ready[1])
                    if oc == 0 and sh["ot_free"]:
                        wt(VE, "S", sh["ot_free"])
                    inst = VE.tensor_scalar(
                        out=s_t[:, slot * TB:(slot + 1) * TB],
                        in0=bk_[:, 0:TB], scalar1=b_out[:, oc:oc + 1],
                        scalar2=None, op0=ALU.add)
                    v_se = sV.inc(inst)
                    rel(bk_, "V", v_se)
                    VE.nop(cycle_cnt=96)
                    gsl = g_t[:, oc * HALF + s_i * TB:oc * HALF + s_i * TB + TB]
                    inst = VE.tensor_mul(out=ot_t[:, oc * TB:(oc + 1) * TB],
                                         in0=s_t[:, slot * TB:(slot + 1) * TB],
                                         in1=gsl)
                    v_out = sV.inc(inst)
                if s_i == 0 and h == 0:
                    tap("s0", s_t[:], (128, 2 * TB), F32, after=("V", sV.n))
                wt(PE_, "V", v_out)
                wt(PE_, "L", prefill_c)
                PE_.dma_start(
                    out=out_d.rearrange("(c p) t -> p c t", p=128)
                    [:, :, tok0:tok0 + TB],
                    in_=ot_t[:].rearrange("p (c t) -> p c t", c=8),
                    accum_op=ALU.add).then_inc(sS.h, 16)
                sS.n += 16
                sh["ot_free"] = sS.n
                yield "B13"

            NW_local = NW
            fgen = front(0)
            for _ in fgen:
                pass
            for s_i in range(NSUB):
                bgen = back(s_i)
                fgen = front(s_i + 1) if s_i + 1 < NSUB else None
                plan = ["B0", "F0", "B1", "F1", "B2", "B3", "B5", "B7",
                        "F2", "B8", "F2n", "F3", "B9.0", "F3n", "F4",
                        "B9.1", "F4n", "B11", "B13"]
                bdone = fdone = False
                for stage in plan:
                    if stage.startswith("F"):
                        if fgen is None or fdone:
                            continue
                        try:
                            got = next(fgen)
                        except StopIteration:
                            fdone = True
                    else:
                        if bdone:
                            continue
                        try:
                            got = next(bgen)
                        except StopIteration:
                            bdone = True
                while not bdone:
                    try:
                        next(bgen)
                    except StopIteration:
                        bdone = True
                if fgen is not None and not fdone:
                    for _ in fgen:
                        pass
            return ("S", sS.n)

        b_done = None
        for h in range(2):
            g_ready, a_done = phase_A(h, b_done)
            if STAGE == "A":
                ga = nc.dram_tensor(f"gA{h}", [128, HALF * NP], BF16,
                                    kind="ExternalOutput").ap()
                wt(SY, "P")
                SY.dma_start(out=ga, in_=g_t[:]).then_inc(sS.h, 16)
                sS.n += 16
                xna = nc.dram_tensor(f"xnA{h}", [128, HALF * NP], BF16,
                                     kind="ExternalOutput").ap()
                SY.dma_start(out=xna, in_=xn_half[:]).then_inc(sS.h, 16)
                sS.n += 16
                cfa = nc.dram_tensor(f"cfA{h}", [16, 4 * TA], BF16,
                                     kind="ExternalOutput").ap()
                SY.dma_start(out=cfa, in_=comb_fb[:]).then_inc(sS.h, 16)
                sS.n += 16
                b_done = ("S", sS.n)
            else:
                b_done = phase_B(h, g_ready, a_done)

        if STAGE == "A":
            wt(SY, "S")
            SY.dma_start(out=out_d, in_=x_d).then_inc(sS.h, 16)
            sS.n += 16
        wt(SY, "S")
    return nc


def kernel(**inputs):
    x = np.asarray(inputs["L0_attn_x"])
    host = _prep_inputs(inputs)
    consts = _host_consts()
    import ml_dtypes
    cin = {}
    for n in ("c_eye", "c_masks", "c_tmat", "c_a", "c_selpair", "c_sele",
              "c_ones1", "c_ones128", "c_oneblk", "c_bc2"):
        cin[n] = consts[n].astype(ml_dtypes.bfloat16)
    cin["c_eyef"] = consts["c_eye"]
    cin["c_ones1f"] = consts["c_ones1"]
    cin["c_w"] = consts["c_w"]
    cin["c_eps"] = consts["c_eps"]

    nc = bass.Bass("TRN2", target_bir_lowering=False)
    build(nc)
    in_maps = []
    for b in range(8):
        m = {"x": np.ascontiguousarray(x[b].T)}
        m.update(host)
        m.update(cin)
        in_maps.append(m)
    res = run_bass_kernel_spmd(nc, in_maps, core_ids=list(range(8)))
    out = np.stack([np.asarray(r["out"]).T for r in res.results], axis=0)
    return out
